# version 06
# baseline (speedup 1.0000x reference)
import numpy as np
import ml_dtypes
import concourse.bass as bass
import concourse.mybir as mybir
from concourse.bass_utils import run_bass_kernel_spmd
from contextlib import ExitStack

F32 = mybir.dt.float32
BF16 = mybir.dt.bfloat16
U32 = mybir.dt.uint32
AF = mybir.ActivationFunctionType
ALU = mybir.AluOpType
AX = mybir.AxisListType

B_, L_, D_ = 4, 4096, 1024
TT = 256
HT = TT // 2
E = 8
EPS = 1e-5
PAIRS = [[0, 1], [2, 3], [4, 5], [6, 7]]


class _Sem:
    __slots__ = ("h", "name", "n")

    def __init__(self, handle, name):
        self.h, self.name, self.n = handle, name, 0


class _FakeInst:
    def then_inc(self, *a, **k):
        return self


class _FakeReg:
    def __enter__(self):
        return self

    def __exit__(self, *a):
        return False


class _FakeEng:
    def register(self, *a, **k):
        return _FakeReg()

    def __getattr__(self, name):
        def f(*a, **k):
            return _FakeInst()
        return f


def build_nc(n_tiles=16, dbg=False):
    nc = bass.Bass("TRN2", target_bir_lowering=False)
    NTG = (n_tiles + 7) // 8

    D = nc.dram_tensor
    x_in = D("x_in", [128, n_tiles, 8, TT], F32, kind="ExternalInput")
    w_in_d = D("w_in", [128, 8, 2048], F32, kind="ExternalInput")
    xprojT_d = D("xprojT", [128, 8, 96], F32, kind="ExternalInput")
    dtprojT_d = D("dtprojT", [64, 8, 128], F32, kind="ExternalInput")
    outprojT_d = D("outprojT", [128, 8, 1024], F32, kind="ExternalInput")
    convw_d = D("convw", [128, 8, 4], F32, kind="ExternalInput")
    convb_d = D("convb", [128, 8], F32, kind="ExternalInput")
    dtb_d = D("dtb", [128, 8], F32, kind="ExternalInput")
    alog_d = D("alog", [128, 8, 16], F32, kind="ExternalInput")
    dp_d = D("dp_in", [128, 8], F32, kind="ExternalInput")
    normw_d = D("normw", [128, 8], F32, kind="ExternalInput")
    centT_d = D("centT", [128, 8, 8], F32, kind="ExternalInput")
    w1_d = D("w1_in", [8, 128, 8, 4096], F32, kind="ExternalInput")
    w2_d = D("w2_in", [8, 128, 32, 1024], F32, kind="ExternalInput")
    b1_d = D("b1_in", [128, 8, 32], F32, kind="ExternalInput")
    b2_d = D("b2_in", [8, 1024], F32, kind="ExternalInput")
    onesc_d = D("onesc", [128, 1], F32, kind="ExternalInput")
    onesr_d = D("onesr", [1, 128], F32, kind="ExternalInput")
    onesrb_d = D("onesrb", [1, 128], BF16, kind="ExternalInput")
    oh_d = D("oh_in", [16, 16, 128], BF16, kind="ExternalInput")
    id_d = D("id_in", [128, 128], F32, kind="ExternalInput")
    iota8_d = D("iota8", [128, 8], F32, kind="ExternalInput")

    y_my = D("y_my", [n_tiles * HT, 1024], F32, kind="ExternalOutput")
    if dbg:
        xattn_dbg = D("xattn_dbg", [128, 8, n_tiles, HT], F32, kind="ExternalOutput")

    rps_dram = D("rps_dram", [n_tiles, 128, 8, HT], F32, kind="Internal")
    ops_dram = D("ops_dram", [n_tiles, 128, 8, HT], F32, kind="Internal")
    dbl_ci = [D(f"dbl_ci{i}", [96, TT], F32, kind="Internal") for i in range(2)]
    dbl_co = [D(f"dbl_co{i}", [96, TT], F32, kind="Internal") for i in range(2)]
    rp_ci = [D(f"rp_ci{i}", [2, 128, 8, HT], F32, kind="Internal") for i in range(2)]
    op_ci = [D(f"op_ci{i}", [2, 128, 8, HT], F32, kind="Internal") for i in range(2)]

    es = ExitStack()

    def sem(name):
        return _Sem(es.enter_context(nc.semaphore(name)), name)

    s_pl = sem("s_pl")
    s_x = [sem("s_x0"), sem("s_x1")]
    s_db = sem("s_db")
    s_rp = sem("s_rp")
    s_op = sem("s_op")
    s_y = sem("s_y")
    s_dbg = sem("s_dbg")
    s_wp = sem("s_wp")
    s_dl = sem("s_dl")
    s_w1 = sem("s_w1")
    s_w1b = sem("s_w1b")
    s_w2 = sem("s_w2")
    s_w2b = sem("s_w2b")
    s_xg = sem("s_xg")
    s_cd = sem("s_cd")
    s_co = sem("s_co")
    s_T = sem("s_T")
    s_S = sem("s_S")
    s_V = sem("s_V")
    s_G = sem("s_G")
    all_sems = [s_pl, s_x[0], s_x[1], s_db, s_rp, s_op, s_y, s_dbg, s_wp,
                s_dl, s_w1, s_w1b, s_w2, s_w2b, s_xg, s_cd, s_co, s_T, s_S, s_V, s_G]

    A = nc.alloc_sbuf_tensor
    onesc = A("onesc_s", [128, 1], F32)
    onesr = A("onesr_s", [1, 128], F32)
    onesrb = A("onesrb_s", [1, 128], BF16)
    idm = A("idm_s", [128, 128], F32)
    pid = A("pid_s", [1, 1], U32)
    statrow = A("statrow_s", [1, 512], F32)
    tiny = A("tiny_s", [1, 512], F32)
    epsc = A("epsc_s", [1, 1], F32)
    iota8 = A("iota8_s", [128, 8], F32)

    p1g = []

    def A1(name, shape, dt):
        g = nc.sbuf_tensor(name, shape, dt)
        t = g.__enter__()
        p1g.append(g)
        return t

    winT = A1("winT", [128, 8, 2048], BF16)
    xprojT = A1("xprojT_s", [128, 8, 96], BF16)
    dtprojT = A1("dtprojT_s", [64, 8, 128], BF16)
    outprojT = A1("outprojT_s", [128, 8, 1024], BF16)
    convw = A1("convw_s", [128, 8, 4], F32)
    convb = A1("convb_s", [128, 8], F32)
    dtb = A1("dtb_s", [128, 8], F32)
    aneg = A1("aneg_s", [128, 8, 16], F32)
    dp = A1("dp_s", [128, 8], F32)
    normw = A1("normw_s", [128, 8], F32)
    oh = A1("oh_s", [16, 16, 128], BF16)
    hstate = A1("hstate_s", [128, 8, 16], F32)
    x_sb = [A1(f"x_sb{i}", [128, 8, TT], F32) for i in range(2)]
    xcbf = [A1(f"xcbf{i}", [128, 8, TT], BF16) for i in range(2)]
    szbf = [A1(f"szbf{i}", [128, 8, TT], BF16) for i in range(2)]
    deltab = [A1(f"deltab{i}", [128, 8, TT], BF16) for i in range(2)]
    wbf = [A1(f"wbf{i}", [128, 8, TT], BF16) for i in range(2)]
    brep = A1("brep_b", [128, 16, TT], BF16)
    crep = A1("crep_b", [128, 16, TT], BF16)
    dtr = A1("dtr_b", [64, TT], BF16)
    braw = A1("braw_b", [16, TT], BF16)
    craw = A1("craw_b", [16, TT], BF16)
    xn = A1("xn_b", [128, 8, TT], F32)
    xn2bf = A1("xn2bf_b", [128, 8, TT], BF16)
    rpre = A1("rpre_b", [128, 8, TT], F32)
    xihalo = A1("xihalo_b", [128, 8, TT + 3], BF16)
    ctmp = [A1(f"ctmp{i}", [128, TT], F32) for i in range(2)]
    sp1 = A1("sp1_b", [128, TT], F32)
    da2 = [A1(f"da2_{i}", [128, 8, TT], F32) for i in range(2)]
    h_b = A1("h_b", [128, 8, TT], BF16)
    yacc = A1("yacc_b", [128, 8, TT], F32)
    ybf = A1("ybf_b", [128, 8, TT], BF16)
    opart = A1("opart_b", [128, 8, 2, HT], F32)
    dbub = [A1(f"dbub{i}", [128, 8, TT], BF16) for i in range(2)]
    convo = xn
    dbu2 = dbub


    ppg = [nc.psum_tensor(f"pp{i}", [128, 512], F32) for i in range(8)]
    pp = [g.__enter__() for g in ppg]
    for g in reversed(ppg):
        g.__exit__(None, None, None)
    for g in reversed(p1g):
        g.__exit__(None, None, None)

    p2g = []

    def A2(name, shape, dt):
        g = nc.sbuf_tensor(name, shape, dt)
        t = g.__enter__()
        p2g.append(g)
        return t

    w1 = A2("w1_b", [128, 8, 4096], BF16)
    w2 = A2("w2_b", [128, 32, 1024], BF16)
    u_bf = A2("u_bf_b", [128, 8, 1024], BF16)
    acc = A2("acc_b", [128, 8, 1024], F32)
    h2 = A2("h2_b", [128, 32, 256], BF16)
    xag = A2("xag_b", [128, 8, 128], F32)
    centT = A2("centT_s", [128, 8, 8], F32)
    b1 = A2("b1_s", [128, 8, 32], F32)
    b2 = A2("b2_s", [8, 1024], BF16)
    gn = A2("gn_s", [128, n_tiles, 8], F32)
    gnT = A2("gnT_s", [8, 128], BF16)
    g8 = A2("g8_s", [8, 256], F32)
    gtm = A2("gtm_s", [128, 6, 8], F32)
    gtmi = A2("gtmi_s", [128, 8], U32)

    qg = ([nc.psum_tensor(f"qa{i}", [128, 512], F32) for i in range(2)]
          + [nc.psum_tensor(f"qb{i}", [128, 1024], F32) for i in range(2)]
          + [nc.psum_tensor(f"qc{i}", [128, 512], F32) for i in range(2)])
    qs = [g.__enter__() for g in qg]
    q_mm1, q_mm2, q_pre = qs[0:2], qs[2:4], qs[4:6]
    for g in reversed(qg):
        g.__exit__(None, None, None)
    for g in reversed(p2g):
        g.__exit__(None, None, None)

    marks = {}

    def run_pass(real):
        for s in all_sems:
            s.n = 0
        bank_last = {}

        def inc(inst, s, k=1):
            s.n += k
            inst.then_inc(s.h, k)
            return inst

        def setmark(name, s):
            marks[name] = (s.h, s.n)

        def wait(eng, name):
            if real:
                h, v = marks[name]
                eng.wait_ge(h, v)

        def tuse(t, bid, consumer_mark):
            if bid in bank_last:
                wait(t, bank_last[bid])
            bank_last[bid] = consumer_mark

        def sec_sync(sy):
            for t_sb, t_dr in [
                (convw, convw_d), (convb, convb_d), (dtb, dtb_d),
                (aneg, alog_d), (dp, dp_d), (normw, normw_d),
                (onesc, onesc_d), (onesr, onesr_d), (onesrb, onesrb_d),
                (idm, id_d), (iota8, iota8_d), (pid, nc.partition_id_tensor),
            ]:
                inc(sy.dma_start(out=t_sb.ap(), in_=t_dr.ap()), s_pl, 16)
            setmark("prolog_ld", s_pl)

            for k in range(n_tiles):
                p = k % 2
                if k >= 2:
                    wait(sy, f"rprev_{k-2}")
                inc(sy.dma_start(out=x_sb[p].ap(), in_=x_in[:, k, :, :]), s_x[p], 16)
                setmark(f"xload_{k}", s_x[p])

                if k >= 2:
                    kk = k - 2
                    wait(sy, f"opartv_{kk}")
                    if kk >= 2:
                        wait(sy, f"opcc_{kk-2}")
                    inc(sy.dma_start(out=op_ci[kk % 2][0], in_=opart[:, :, 0, :]), s_op, 16)
                    inc(sy.dma_start(out=op_ci[kk % 2][1], in_=opart[:, :, 1, :]), s_op, 16)
                    setmark(f"opstore_{kk}", s_op)

                wait(sy, f"rprev_{k}")
                if k >= 2:
                    wait(sy, f"rpcc_{k-2}")
                inc(sy.dma_start(out=rp_ci[p][0], in_=rpre[:, :, 0:HT]), s_rp, 16)
                inc(sy.dma_start(out=rp_ci[p][1], in_=rpre[:, :, HT:TT]), s_rp, 16)
                setmark(f"rprestore_{k}", s_rp)

                wait(sy, f"dblmy_{k}")
                if k >= 2:
                    wait(sy, f"dblcc_{k-2}")
                inc(sy.dma_start(out=dbl_ci[p].ap(), in_=ctmp[0][0:96, :]), s_db, 16)
                setmark(f"dblstore_{k}", s_db)

            for kk in (n_tiles - 2, n_tiles - 1):
                if kk >= 0:
                    wait(sy, f"opartv_{kk}")
                    if kk >= 2:
                        wait(sy, f"opcc_{kk-2}")
                    inc(sy.dma_start(out=op_ci[kk % 2][0], in_=opart[:, :, 0, :]), s_op, 16)
                    inc(sy.dma_start(out=op_ci[kk % 2][1], in_=opart[:, :, 1, :]), s_op, 16)
                    setmark(f"opstore_{kk}", s_op)
            wait(sy, "p1v")
            wait(sy, f"op_{n_tiles-1}_{7}")
            inc(sy.dma_start(out=centT.ap(), in_=centT_d.ap()), s_pl, 16)
            inc(sy.dma_start(out=b1.ap(), in_=b1_d.ap()), s_pl, 16)
            setmark("p2consts", s_pl)
            for g_i in range(NTG):
                for gg in range(8):
                    gr = g_i * 8 + gg
                    if gr >= n_tiles:
                        continue
                    if dbg:
                        wait(sy, f"xagv_{gr}")
                        inc(sy.dma_start(out=xattn_dbg[:, :, gr, :], in_=xag.ap()),
                            s_dbg, 16)
                        setmark(f"xadbg_{gr}", s_dbg)
                wait(sy, f"accdone_{g_i}")
                for gg in range(8):
                    gr = g_i * 8 + gg
                    if gr >= n_tiles:
                        continue
                    inc(sy.dma_start(out=y_my[gr * HT:(gr + 1) * HT, :],
                                     in_=acc[:, gg, :]), s_y, 16)
                setmark(f"ystore_{g_i}", s_y)
            if real:
                for s_ in (s_db, s_rp, s_op, s_y, s_dbg):
                    if s_.n:
                        sy.wait_ge(s_.h, s_.n)

        def sec_gp(gp):
            for t_sb, t_dr in [(winT, w_in_d), (xprojT, xprojT_d),
                               (outprojT, outprojT_d), (dtprojT, dtprojT_d),
                               (oh, oh_d)]:
                inc(gp.dma_start(out=t_sb.ap(), in_=t_dr.ap()), s_wp, 16)
            setmark("prolog_w", s_wp)

            for k in range(n_tiles):
                p = k % 2
                if k >= 2:
                    kk = k - 2
                    wait(gp, f"opstore_{kk}")
                    inc(gp.collective_compute(
                        "ReduceScatter", ALU.add, replica_groups=PAIRS,
                        ins=[op_ci[kk % 2].ap().opt()],
                        outs=[ops_dram[kk].opt()]), s_co, 1)
                    setmark(f"opcc_{kk}", s_co)
                def gp_dbu(kk1, n0, n1):
                    for n in range(n0, n1):
                        wait(gp, f"wk_{kk1}")
                        wait(gp, f"bbev_{kk1}_{n}")
                        if n >= 2:
                            wait(gp, f"scnc_{kk1}_{n-2}")
                        for c in range(8):
                            vv = gp.tensor_mul(dbu2[n % 2][:, c, :],
                                               wbf[kk1 % 2][:, c, :], brep[:, n, :])
                        gp.drain()
                        inc(vv, s_G)
                        setmark(f"dbu_{kk1}_{n}", s_G)
                if k >= 1:
                    gp_dbu(k - 1, 0, 8)
                wait(gp, f"rprestore_{k}")
                inc(gp.collective_compute(
                    "ReduceScatter", ALU.add, replica_groups=PAIRS,
                    ins=[rp_ci[p].ap().opt()], outs=[rps_dram[k].opt()]), s_cd, 1)
                setmark(f"rpcc_{k}", s_cd)
                wait(gp, f"dblstore_{k}")
                inc(gp.collective_compute(
                    "AllReduce", ALU.add, replica_groups=PAIRS,
                    ins=[dbl_ci[p].ap().opt()], outs=[dbl_co[p].ap().opt()]), s_cd, 1)
                setmark(f"dblcc_{k}", s_cd)
                wait(gp, f"dblcc_{k}")
                if k >= 2:
                    wait(gp, f"scandone_{k-2}")
                inc(gp.dma_start(out=dtr.ap(), in_=dbl_co[p][0:64, :]), s_dl, 16)
                inc(gp.dma_start(out=braw.ap(), in_=dbl_co[p][64:80, :]), s_dl, 16)
                inc(gp.dma_start(out=craw.ap(), in_=dbl_co[p][80:96, :]), s_dl, 16)
                setmark(f"dblld_{k}", s_dl)
                if k >= 1:
                    gp_dbu(k - 1, 8, 16)
            kk1 = n_tiles - 1
            for n in range(16):
                wait(gp, f"wk_{kk1}")
                wait(gp, f"bbev_{kk1}_{n}")
                if n >= 2:
                    wait(gp, f"scnc_{kk1}_{n-2}")
                for c in range(8):
                    vv = gp.tensor_mul(dbu2[n % 2][:, c, :],
                                       wbf[kk1 % 2][:, c, :], brep[:, n, :])
                gp.drain()
                inc(vv, s_G)
                setmark(f"dbu_{kk1}_{n}", s_G)
            del gp_dbu

            for kk in (n_tiles - 2, n_tiles - 1):
                if kk >= 0:
                    wait(gp, f"opstore_{kk}")
                    inc(gp.collective_compute(
                        "ReduceScatter", ALU.add, replica_groups=PAIRS,
                        ins=[op_ci[kk % 2].ap().opt()],
                        outs=[ops_dram[kk].opt()]), s_co, 1)
                    setmark(f"opcc_{kk}", s_co)
            wait(gp, "p1v")
            wait(gp, f"rprestore_{n_tiles-1}")
            wait(gp, f"opstore_{n_tiles-1}")
            inc(gp.dma_start(out=b2.ap(), in_=b2_d.ap()), s_wp, 16)
            setmark("p2b2ld", s_wp)
            for g_i in range(NTG):
                for gg in range(8):
                    gr = g_i * 8 + gg
                    if gr >= n_tiles:
                        continue
                    wait(gp, f"opcc_{gr}")
                    if gr >= 1:
                        wait(gp, f"xagT_{gr-1}")
                    if dbg and gr >= 1:
                        wait(gp, f"xadbg_{gr-1}")
                    wait(gp, f"rpcc_{gr}")
                    inc(gp.dma_start(out=xag.ap(), in_=rps_dram[gr]), s_xg, 16)
                    if real:
                        gp.wait_ge(s_xg.h, s_xg.n)
                    inc(gp.dma_start(out=xag.ap(), in_=ops_dram[gr],
                                     accum_op=ALU.add), s_xg, 16)
                    setmark(f"xagv_{gr}", s_xg)
                for e in range(E):
                    first = (g_i == 0 and e == 0)
                    prev = f"{g_i}_{e-1}" if e > 0 else f"{g_i-1}_{E-1}"
                    if first:
                        pass
                    else:
                        wait(gp, f"mm1h_{prev}")
                    inc(gp.dma_start(out=w1[:, :, 0:2048], in_=w1_d[e, :, :, 0:2048]),
                        s_w1, 16)
                    setmark(f"w1a_{g_i}_{e}", s_w1)
                    if not first:
                        wait(gp, f"mm1d_{prev}")
                    inc(gp.dma_start(out=w1[:, :, 2048:4096], in_=w1_d[e, :, :, 2048:4096]),
                        s_w1b, 16)
                    setmark(f"w1_{g_i}_{e}", s_w1b)
                    if not first:
                        wait(gp, f"mm2d_{prev}")
                    inc(gp.dma_start(out=w2[:, :, 0:512], in_=w2_d[e, :, :, 0:512]),
                        s_w2, 16)
                    setmark(f"w2_{g_i}_{e}", s_w2)
                    inc(gp.dma_start(out=w2[:, :, 512:1024], in_=w2_d[e, :, :, 512:1024]),
                        s_w2b, 16)
                    setmark(f"w2b_{g_i}_{e}", s_w2b)

        def t_front(t, k):
            p = k % 2
            wait(t, f"xload_{k}")
            tuse(t, 0, f"rmsrow_{k}")
            for c in range(8):
                mm = t.matmul(pp[0][0:1, 0:TT], onesc[:, 0:1], x_sb[p][:, c, :],
                              start=(c == 0), stop=(c == 7))
            inc(mm, s_T)
            setmark(f"sumx_{k}", s_T)
            wait(t, f"xsq_{k}")
            tuse(t, 1, f"xn2v_{k}")
            for c in range(8):
                mm = t.matmul(pp[1][0:1, 0:TT], onesc[:, 0:1], xn[:, c, :],
                              start=(c == 0), stop=(c == 7))
            inc(mm, s_T)
            setmark(f"sumx2_{k}", s_T)
            wait(t, f"statrow_{k}")
            tuse(t, 2, f"lnapply_{k}")
            inc(t.matmul(pp[2][:, 0:512], onesr[0:1, :], statrow[0:1, 0:512],
                         start=True, stop=True), s_T)
            setmark(f"bcast1_{k}", s_T)
            wait(t, f"xnsq_{k}")
            for c in range(8):
                mm = t.matmul(pp[0][0:1, TT:2 * TT], onesc[:, 0:1], rpre[:, c, :],
                              start=(c == 0), stop=(c == 7))
            inc(mm, s_T)
            setmark(f"sumxn2_{k}", s_T)
            wait(t, f"rrmsrow_{k}")
            inc(t.matmul(pp[1][:, TT:2 * TT], onesr[0:1, :], statrow[0:1, 256:256 + TT],
                         start=True, stop=True), s_T)
            setmark(f"bcast2_{k}", s_T)
            wait(t, f"xn2v_{k}")
            for f in range(16):
                bid = 3 + (f % 2)
                tuse(t, bid, f"ipev_{k}_{f}")
                for c in range(8):
                    mm = t.matmul(pp[bid][:, 0:TT], winT[:, c, f * 128:(f + 1) * 128],
                                  xn2bf[:, c, :], start=(c == 0), stop=(c == 7))
                inc(mm, s_T)
                setmark(f"ip_{k}_{f}", s_T)

        def t_tail_prev(t, k):
            p = k % 2
            wait(t, f"dblld_{k}")
            for c in range(8):
                bid = 5 + (c % 2)
                tuse(t, bid, f"dtev_{k}_{c}")
                inc(t.matmul(pp[bid][:, 0:TT], dtprojT[:, c, :], dtr[:, :],
                             start=True, stop=True), s_T)
                setmark(f"dt_{k}_{c}", s_T)
            for n in range(16):
                bid = 5 + (n % 2)
                tuse(t, bid, f"bbev_{k}_{n}")
                inc(t.matmul(pp[bid][:, 0:TT], oh[:, n, :], braw[:, :],
                             start=True, stop=True), s_T)
                setmark(f"bb_{k}_{n}", s_T)
            for n in range(16):
                bid = 5 + (n % 2)
                tuse(t, bid, f"ccev_{k}_{n}")
                inc(t.matmul(pp[bid][:, 0:TT], oh[:, n, :], craw[:, :],
                             start=True, stop=True), s_T)
                setmark(f"cb_{k}_{n}", s_T)

        def t_dbl(t, k):
            p = k % 2
            wait(t, f"xcbfs_{k}")
            tuse(t, 7, f"dblmy_{k}")
            for c in range(8):
                mm = t.matmul(pp[7][0:96, 0:TT], xprojT[:, c, :], xcbf[p][:, c, :],
                              start=(c == 0), stop=(c == 7))
            inc(mm, s_T)
            setmark(f"dblps_{k}", s_T)

        def t_opart(t, k):
            wait(t, f"ydone_{k}")
            for dm in range(8):
                bid = 3 + (dm % 2)
                tuse(t, bid, f"opev_{k}_{dm}")
                for c in range(8):
                    mm = t.matmul(pp[bid][:, 0:TT], outprojT[:, c, dm * 128:(dm + 1) * 128],
                                  ybf[:, c, :], start=(c == 0), stop=(c == 7))
                inc(mm, s_T)
                setmark(f"op_{k}_{dm}", s_T)

        def t_p2_pre(t, gr):
            wait(t, f"xagv_{gr}")
            if gr == 0:
                wait(t, "p1v")
            tuse(t, 10, f"p2gnTv_{gr}")
            for c in range(8):
                mm = t.matmul(q_pre[0][0:1, 0:HT], onesc[:, 0:1], xag[:, c, :],
                              start=(c == 0), stop=(c == 7))
            inc(mm, s_T)
            setmark(f"p2sumx_{gr}", s_T)
            wait(t, f"p2xsq_{gr}")
            for c in range(8):
                mm = t.matmul(q_pre[0][0:1, HT:2 * HT], onesc[:, 0:1],
                              acc[:, gr % 8, c * 128:(c + 1) * 128],
                              start=(c == 0), stop=(c == 7))
            inc(mm, s_T)
            setmark(f"p2sumx2_{gr}", s_T)
            wait(t, f"p2statrow_{gr}")
            tuse(t, 11, f"p2lnapply_{gr}")
            inc(t.matmul(q_pre[1][:, 0:2 * HT], onesr[0:1, :], statrow[0:1, 0:2 * HT],
                         start=True, stop=True), s_T)
            setmark(f"p2bcast_{gr}", s_T)
            wait(t, f"p2lnapply_{gr}")
            if gr == 0:
                wait(t, "p2consts")
            for c in range(8):
                mm = t.matmul(q_pre[0][0:8, 2 * HT:3 * HT], centT[:, c, :],
                              acc[:, gr % 8, c * 128:(c + 1) * 128],
                              start=(c == 0), stop=(c == 7))
            inc(mm, s_T)
            setmark(f"p2glog_{gr}", s_T)
            wait(t, f"p2g8_{gr}")
            inc(t.transpose(q_pre[0][:, 3 * HT:3 * HT + 8], g8[:, 128:256],
                            idm[0:8, 0:8]), s_T)
            setmark(f"p2gt_{gr}", s_T)
            wait(t, f"p2gn_{gr}")
            inc(t.transpose(q_pre[0][0:8, HT:2 * HT], gn[:, gr, :], idm[:, :]), s_T)
            setmark(f"p2gnT_{gr}", s_T)
            wait(t, f"p2gnTv_{gr}")
            if gr == 0:
                wait(t, "p2b2ld")
            tuse(t, 12, f"p2ai_{gr}_{7}")
            t.matmul(q_mm2[0][:, 0:512], gnT[:, :], b2[:, 0:512],
                     start=True, stop=True)
            inc(t.matmul(q_mm2[0][:, 512:1024], gnT[:, :], b2[:, 512:1024],
                         start=True, stop=True), s_T)
            setmark(f"p2b2_{gr}", s_T)
            for dm in range(8):
                bq = q_pre[dm % 2]
                if dm == 0:
                    wait(t, f"p2gnTv_{gr}")
                elif dm == 1:
                    wait(t, f"p2lnapply_{gr}")
                else:
                    wait(t, f"p2ai_{gr}_{dm-2}")
                inc(t.transpose(bq[:, 2 * HT:3 * HT], xag[:, dm, :], idm[:, :]), s_T)
                setmark(f"p2tr_{gr}_{dm}", s_T)
            setmark(f"xagT_{gr}", s_T)
            bank_last[10] = f"p2ai_{gr}_{6}"
            bank_last[11] = f"p2ai_{gr}_{7}"

        def t_p2_expert(t, g_i, e):
            gbase = g_i * 8
            tts = [tt for tt in range(4) if gbase + tt * 2 < n_tiles]
            wait(t, f"w1a_{g_i}_{e}")
            for tt in tts:
                if e == 0:
                    for g2 in range(2):
                        if gbase + tt * 2 + g2 < n_tiles:
                            wait(t, f"p2uf_{gbase + tt * 2 + g2}")
                for m in range(32):
                    bid = 14 + (m % 2)
                    if m == 16 and tt == tts[0]:
                        wait(t, f"w1_{g_i}_{e}")
                    tuse(t, bid, f"geev_{g_i}_{e}_{tt}_{m}")
                    for c in range(8):
                        mm = t.matmul(q_mm1[m % 2][:, 0:256],
                                      w1[:, c, m * 128:(m + 1) * 128],
                                      u_bf[:, c, tt * 256:(tt + 1) * 256],
                                      start=(c == 0), stop=(c == 7))
                    inc(mm, s_T)
                    setmark(f"mm1_{g_i}_{e}_{tt}_{m}", s_T)
                    if m == 15 and tt == tts[-1]:
                        setmark(f"mm1h_{g_i}_{e}", s_T)
                if tt == tts[-1]:
                    setmark(f"mm1d_{g_i}_{e}", s_T)
                wait(t, f"gedone_{g_i}_{e}_{tt}")
                if tt == tts[0]:
                    wait(t, f"w2_{g_i}_{e}")
                for g2 in range(2):
                    gg = tt * 2 + g2
                    if gbase + gg >= n_tiles:
                        continue
                    bid = 12 + g2
                    tuse(t, bid, f"accev_{g_i}_{e}_{gg}")
                    for half in range(2):
                        if half == 1 and tt == tts[0]:
                            wait(t, f"w2b_{g_i}_{e}")
                        for c in range(32):
                            mm = t.matmul(q_mm2[g2][:, half * 512:(half + 1) * 512],
                                          h2[:, c, g2 * 128:(g2 + 1) * 128],
                                          w2[:, c, half * 512:(half + 1) * 512],
                                          start=(c == 0), stop=(c == 31))
                    inc(mm, s_T)
                    setmark(f"mm2_{g_i}_{e}_{gg}", s_T)
            setmark(f"mm2d_{g_i}_{e}", s_T)

        def sec_tensor(t):
            wait(t, "prolog_ld")
            wait(t, "prolog_w")
            for k in range(n_tiles):
                if k >= 1:
                    t_tail_prev(t, k - 1)
                if k >= 2:
                    t_opart(t, k - 2)
                t_front(t, k)
                t_dbl(t, k)
            t_tail_prev(t, n_tiles - 1)
            if n_tiles >= 2:
                t_opart(t, n_tiles - 2)
            t_opart(t, n_tiles - 1)
            for g_i in range(NTG):
                for gg in range(8):
                    gr = g_i * 8 + gg
                    if gr < n_tiles:
                        t_p2_pre(t, gr)
                for e in range(E):
                    t_p2_expert(t, g_i, e)

        def s_front_a(s, k):
            p = k % 2
            wait(s, f"xload_{k}")
            if k == 0:
                wait(s, "anegdone")
            if k >= 1:
                wait(s, f"convt_{k-1}")
            for c in range(8):
                act = s.activation(xn[:, c, :], x_sb[p][:, c, :], AF.Square)
            inc(act, s_S)
            setmark(f"xsq_{k}", s_S)
            wait(s, f"sumx_{k}")
            wait(s, f"sumx2_{k}")
            if k >= 1:
                wait(s, f"statrow_{k-1}")
                wait(s, f"rrmsrow_{k-1}")
            s.activation(statrow[0:1, 0:TT], pp[0][0:1, 0:TT], AF.Copy, scale=1.0 / 1024)
            s.activation(tiny[0:1, 0:TT], pp[1][0:1, 0:TT], AF.Copy, scale=1.0 / 1024)
            s.drain()
            inc(s.activation(tiny[0:1, TT:2 * TT], statrow[0:1, 0:TT], AF.Square), s_S)
            setmark(f"stA_{k}", s_S)
            wait(s, f"stB_{k}")
            inc(s.activation(tiny[0:1, 0:TT], tiny[0:1, TT:2 * TT], AF.Sqrt, bias=epsc[0:1, 0:1]),
                s_S)
            setmark(f"stC_{k}", s_S)
            wait(s, f"lnapply_{k}")
            if k >= 1:
                wait(s, f"rprestore_{k-1}")
            for c in range(8):
                act = s.activation(rpre[:, c, :], xn[:, c, :], AF.Square)
            inc(act, s_S)
            setmark(f"xnsq_{k}", s_S)
            wait(s, f"sumxn2_{k}")
            wait(s, f"bcast1_{k}")
            s.activation(tiny[0:1, TT:2 * TT], pp[0][0:1, TT:2 * TT], AF.Copy,
                         scale=1.0 / 1024)
            s.drain()
            inc(s.activation(tiny[0:1, 0:TT], tiny[0:1, TT:2 * TT], AF.Sqrt,
                             bias=epsc[0:1, 0:1]), s_S)
            setmark(f"rmsrow_{k}", s_S)
        def s_front_b(s, k):
            p = k % 2
            if k >= 2:
                wait(s, f"ydone_{k-2}")
            for f in range(16):
                bid = 3 + (f % 2)
                wait(s, f"ip_{k}_{f}")
                if f == 0:
                    wait(s, f"halo_{k}")
                if f < 8:
                    act = s.activation(xihalo[:, f, 3:3 + TT], pp[bid][:, 0:TT], AF.Copy)
                else:
                    act = s.activation(szbf[p][:, f - 8, :], pp[bid][:, 0:TT], AF.Silu)
                inc(act, s_S)
                setmark(f"ipev_{k}_{f}", s_S)
        def s_silu(s, k):
            p = k % 2
            wait(s, f"convt_{k}")
            for c in range(8):
                act = s.activation(xcbf[p][:, c, :], convo[:, c, :], AF.Silu,
                                   bias=convb[:, c:c + 1])
            inc(act, s_S)
            setmark(f"xcbfs_{k}", s_S)

        def s_softplus(s, k):
            p = k % 2
            if k >= 2:
                wait(s, f"wk_{k-2}")
            for c in range(8):
                bid = 5 + (c % 2)
                wait(s, f"dt_{k}_{c}")
                act = s.activation(sp1[:, :], pp[bid][:, 0:TT], AF.Exp,
                                   bias=dtb[:, c:c + 1])
                inc(act, s_S)
                setmark(f"dtev_{k}_{c}", s_S)
                s.drain()
                act = s.activation(deltab[p][:, c, :], sp1[:, :], AF.Ln, bias=1.0)
                s.drain()
            inc(act, s_S)
            setmark(f"delta_{k}", s_S)
            if k >= 1:
                wait(s, f"dbu_{k-1}_{15}")
                wait(s, f"scn_{k-1}_{15}")
            for n in range(16):
                bid = 5 + (n % 2)
                wait(s, f"bb_{k}_{n}")
                inc(s.activation(brep[:, n, :], pp[bid][:, 0:TT], AF.Copy), s_S)
                setmark(f"bbev_{k}_{n}", s_S)
            for n in range(16):
                bid = 5 + (n % 2)
                wait(s, f"cb_{k}_{n}")
                inc(s.activation(crep[:, n, :], pp[bid][:, 0:TT], AF.Copy), s_S)
                setmark(f"ccev_{k}_{n}", s_S)

        def s_daexp_part(s, k, n0, n1):
            p = k % 2
            for n in range(n0, n1):
                if n >= 2:
                    wait(s, f"scn_{k}_{n-2}")
                elif k >= 1:
                    wait(s, f"scn_{k-1}_{14+n}")
                for c in range(8):
                    act = s.activation(da2[n % 2][:, c, :], deltab[p][:, c, :], AF.Exp,
                                       scale=aneg[:, c, n:n + 1])
                inc(act, s_S)
                setmark(f"da_{k}_{n}", s_S)

        def sec_scalar(s):
            wait(s, "prolog_ld")
            wait(s, "epscv")
            inc(s.activation(xn[:, :, 0:16], aneg[:, :, :], AF.Exp), s_S)
            setmark("anegexp", s_S)
            for k in range(n_tiles):
                if k >= 1:
                    s_softplus(s, k - 1)
                    s_daexp_part(s, k - 1, 0, 2)
                s_front_a(s, k)
                if k >= 1:
                    s_daexp_part(s, k - 1, 2, 6)
                s_front_b(s, k)
                if k >= 1:
                    s_daexp_part(s, k - 1, 6, 10)
                s_silu(s, k)
                if k >= 1:
                    s_daexp_part(s, k - 1, 10, 16)
            s_softplus(s, n_tiles - 1)
            s_daexp_part(s, n_tiles - 1, 0, 16)
            for g_i in range(NTG):
                for gg in range(8):
                    gr = g_i * 8 + gg
                    if gr >= n_tiles:
                        continue
                    wait(s, f"xagv_{gr}")
                    if gr == 0:
                        wait(s, "p1v")
                    if gr >= 1:
                        wait(s, f"p2uf_{gr-1}")
                        wait(s, f"p2glog_{gr-1}")
                        wait(s, f"p2statrow_{gr-1}")
                    if g_i >= 1 and gg == 0:
                        wait(s, f"ystore_{g_i-1}")
                    for c in range(8):
                        act = s.activation(acc[:, gg, c * 128:(c + 1) * 128],
                                           xag[:, c, :], AF.Square)
                    inc(act, s_S)
                    setmark(f"p2xsq_{gr}", s_S)
                    wait(s, f"p2sumx_{gr}")
                    wait(s, f"p2sumx2_{gr}")
                    s.activation(statrow[0:1, 0:HT], q_pre[0][0:1, 0:HT], AF.Copy,
                                 scale=1.0 / 1024)
                    s.activation(tiny[0:1, 0:HT], q_pre[0][0:1, HT:2 * HT], AF.Copy,
                                 scale=1.0 / 1024)
                    s.drain()
                    inc(s.activation(tiny[0:1, HT:2 * HT], statrow[0:1, 0:HT],
                                     AF.Square), s_S)
                    setmark(f"p2stA_{gr}", s_S)
                    wait(s, f"p2stB_{gr}")
                    inc(s.activation(tiny[0:1, 0:HT], tiny[0:1, HT:2 * HT], AF.Sqrt,
                                     bias=epsc[0:1, 0:1]), s_S)
                    setmark(f"p2stC_{gr}", s_S)
                    wait(s, f"p2negl_{gr}")
                    inc(s.activation(g8[:, 128:256], g8[:, 0:128], AF.Exp), s_S)
                    setmark(f"p2ge_{gr}", s_S)
                for e in range(E):
                    gbase = g_i * 8
                    for tt in range(4):
                        if gbase + tt * 2 >= n_tiles:
                            continue
                        for m in range(32):
                            wait(s, f"mm1_{g_i}_{e}_{tt}_{m}")
                            if g_i == 0 and e == 0 and tt == 0 and m == 0:
                                wait(s, "p2consts")
                            act = s.activation(h2[:, m, :], q_mm1[m % 2][:, 0:256],
                                               AF.Gelu, bias=b1[:, e, m:m + 1])
                            inc(act, s_S)
                            setmark(f"geev_{g_i}_{e}_{tt}_{m}", s_S)
                        setmark(f"gedone_{g_i}_{e}_{tt}", s_S)

        def v_front(v, k):
            p = k % 2
            if k == 0:
                inc(v.memset(xihalo[:, :, 0:3], 0.0), s_V)
            else:
                wait(v, f"ipev_{k-1}_{7}")
                inc(v.tensor_copy(xihalo[:, :, 0:3], xihalo[:, :, TT:TT + 3]), s_V)
            setmark(f"halo_{k}", s_V)
            wait(v, f"stA_{k}")
            inc(v.tensor_sub(tiny[0:1, TT:2 * TT], tiny[0:1, 0:TT],
                             tiny[0:1, TT:2 * TT]), s_V)
            setmark(f"stB_{k}", s_V)
            wait(v, f"stC_{k}")
            inc(v.reciprocal(statrow[0:1, TT:2 * TT], tiny[0:1, 0:TT]), s_V)
            setmark(f"statrow_{k}", s_V)
            wait(v, f"bcast1_{k}")
            wait(v, f"sumx2_{k}")
            if k >= 1:
                wait(v, f"xcbfs_{k-1}")
            for c in range(8):
                v.tensor_sub(xn[:, c, :], x_sb[p][:, c, :], pp[2][:, 0:TT])
            v.drain()
            for c in range(8):
                vv = v.tensor_mul(xn[:, c, :], xn[:, c, :], pp[2][:, TT:2 * TT])
            inc(vv, s_V)
            setmark(f"lnapply_{k}", s_V)
            wait(v, f"rmsrow_{k}")
            inc(v.reciprocal(statrow[0:1, 256:256 + TT], tiny[0:1, 0:TT]), s_V)
            setmark(f"rrmsrow_{k}", s_V)
            wait(v, f"bcast2_{k}")
            for c in range(8):
                vv = v.scalar_tensor_tensor(xn2bf[:, c, :], xn[:, c, :],
                                            normw[:, c:c + 1], pp[1][:, TT:2 * TT],
                                            ALU.mult, ALU.mult)
            inc(vv, s_V)
            setmark(f"xn2v_{k}", s_V)
            wait(v, f"sumxn2_{k}")
            for c in range(8):
                v.tensor_add(rpre[:, c, :], x_sb[p][:, c, :], xn[:, c, :])
            v.drain()
            for c in range(8):
                vv = v.tensor_scalar(rpre[:, c, :], rpre[:, c, :], 0.5, None, ALU.mult)
            inc(vv, s_V)
            setmark(f"rprev_{k}", s_V)
        def v_front2(v, k):
            p = k % 2
            if k >= 1:
                wait(v, f"dblstore_{k-1}")
            for c in range(8):
                wait(v, f"ipev_{k}_{c}")
                v.tensor_scalar(ctmp[0][:, :], xihalo[:, c, 0:TT], convw[:, c, 0:1],
                                None, ALU.mult)
                v.drain()
                v.scalar_tensor_tensor(ctmp[1][:, :], xihalo[:, c, 1:1 + TT],
                                       convw[:, c, 1:2], ctmp[0][:, :], ALU.mult, ALU.add)
                v.drain()
                v.scalar_tensor_tensor(ctmp[0][:, :], xihalo[:, c, 2:2 + TT],
                                       convw[:, c, 2:3], ctmp[1][:, :], ALU.mult, ALU.add)
                v.drain()
                vv = v.scalar_tensor_tensor(convo[:, c, :], xihalo[:, c, 3:3 + TT],
                                            convw[:, c, 3:4], ctmp[0][:, :],
                                            ALU.mult, ALU.add)
                v.drain()
            inc(vv, s_V)
            setmark(f"convt_{k}", s_V)
            wait(v, f"dblps_{k}")
            inc(v.tensor_copy(ctmp[0][0:96, :], pp[7][0:96, 0:TT]), s_V)
            setmark(f"dblmy_{k}", s_V)


        def v_scan_head(v, k):
            p = k % 2
            wait(v, f"delta_{k}")
            for c in range(8):
                vv = v.tensor_mul(wbf[p][:, c, :], deltab[p][:, c, :], xcbf[p][:, c, :])
            inc(vv, s_V)
            setmark(f"wk_{k}", s_V)
            for c in range(8):
                v.tensor_scalar(yacc[:, c, :], xcbf[p][:, c, :], dp[:, c:c + 1],
                                None, ALU.mult)
            v.drain()

        def v_scan_part(v, k, n0, n1):
            p = k % 2
            for n in range(n0, n1):
                pn = n % 2
                wait(v, f"da_{k}_{n}")
                wait(v, f"dbu_{k}_{n}")
                for c in range(8):
                    v.tensor_tensor_scan(h_b[:, c, :], da2[pn][:, c, :],
                                         dbu2[pn][:, c, :],
                                         hstate[:, c, n:n + 1], ALU.mult, ALU.add)
                v.drain()
                v.tensor_copy(hstate[:, :, n:n + 1], h_b[:, :, TT - 1:TT])
                for c in range(8):
                    vv = v.tensor_mul(dbu2[pn][:, c, :], h_b[:, c, :], crep[:, n, :])
                v.drain()
                inc(vv, s_V)
                setmark(f"scn_{k}_{n}", s_V)
                for c in range(8):
                    vv = v.tensor_add(yacc[:, c, :], yacc[:, c, :], dbu2[pn][:, c, :])
                v.drain()
                inc(vv, s_V)
                setmark(f"scnc_{k}_{n}", s_V)

        def v_scan_tail(v, k):
            p = k % 2
            if k >= 1:
                wait(v, f"op_{k-1}_{7}")
            for c in range(8):
                vv = v.tensor_mul(ybf[:, c, :], yacc[:, c, :], szbf[p][:, c, :])
            inc(vv, s_V)
            setmark(f"ydone_{k}", s_V)
            setmark(f"scandone_{k}", s_V)

        def v_opart_prev(v, k):
            for dm in range(8):
                bid = 3 + (dm % 2)
                wait(v, f"op_{k}_{dm}")
                if dm == 0 and k >= 1:
                    wait(v, f"opstore_{k-1}")
                v.tensor_copy(opart[:, dm, 0, :], pp[bid][:, 0:HT])
                inc(v.tensor_copy(opart[:, dm, 1, :], pp[bid][:, HT:TT]), s_V)
                setmark(f"opev_{k}_{dm}", s_V)
            setmark(f"opartv_{k}", s_V)

        def sec_vector(v):
            inc(v.memset(epsc[0:1, 0:1], EPS), s_V)
            setmark("epscv", s_V)
            wait(v, "anegexp")
            inc(v.tensor_scalar(aneg[:, :, :], xn[:, :, 0:16], -1.0, None, ALU.mult),
                s_V)
            setmark("anegdone", s_V)
            inc(v.memset(hstate.ap(), 0.0), s_V)
            for k in range(n_tiles):
                if k >= 2:
                    v_opart_prev(v, k - 2)
                v_front(v, k)
                if k >= 1:
                    v_scan_head(v, k - 1)
                    v_scan_part(v, k - 1, 0, 8)
                v_front2(v, k)
                if k >= 1:
                    v_scan_part(v, k - 1, 8, 16)
                    v_scan_tail(v, k - 1)
            if n_tiles >= 2:
                v_opart_prev(v, n_tiles - 2)
            v_scan_head(v, n_tiles - 1)
            v_scan_part(v, n_tiles - 1, 0, 16)
            v_scan_tail(v, n_tiles - 1)
            v_opart_prev(v, n_tiles - 1)
            inc(v.engine_nop(), s_V)
            setmark("p1v", s_V)
            wait(v, f"rprestore_{n_tiles-1}")
            wait(v, f"opstore_{n_tiles-1}")
            if n_tiles % 8:
                v.memset(u_bf.ap(), 0.0)
                v.drain()
            for g_i in range(NTG):
                for gg in range(8):
                    gr = g_i * 8 + gg
                    if gr >= n_tiles:
                        continue
                    wait(v, f"xagv_{gr}")
                    wait(v, f"p2stA_{gr}")
                    inc(v.tensor_sub(tiny[0:1, HT:2 * HT], tiny[0:1, 0:HT],
                                     tiny[0:1, HT:2 * HT]), s_V)
                    setmark(f"p2stB_{gr}", s_V)
                    wait(v, f"p2stC_{gr}")
                    inc(v.reciprocal(statrow[0:1, HT:2 * HT], tiny[0:1, 0:HT]), s_V)
                    setmark(f"p2statrow_{gr}", s_V)
                    wait(v, f"p2bcast_{gr}")
                    wait(v, f"p2sumx2_{gr}")
                    for c in range(8):
                        v.tensor_sub(acc[:, gg, c * 128:(c + 1) * 128], xag[:, c, :],
                                     q_pre[1][:, 0:HT])
                    v.drain()
                    for c in range(8):
                        vv = v.tensor_mul(acc[:, gg, c * 128:(c + 1) * 128],
                                          acc[:, gg, c * 128:(c + 1) * 128],
                                          q_pre[1][:, HT:2 * HT])
                    inc(vv, s_V)
                    setmark(f"p2lnapply_{gr}", s_V)
                    v.drain()
                    if g_i >= 1 and gg == 0:
                        wait(v, f"mm1d_{g_i-1}_{E-1}")
                    for c in range(8):
                        vv = v.tensor_copy(u_bf[:, c, gg * 128:(gg + 1) * 128],
                                           acc[:, gg, c * 128:(c + 1) * 128])
                    inc(vv, s_V)
                    setmark(f"p2uf_{gr}", s_V)
                    wait(v, f"p2glog_{gr}")
                    v.tensor_scalar(g8[:, 0:128], q_pre[0][0:8, 2 * HT:3 * HT],
                                    -1.0, 30.0, ALU.mult, ALU.min)
                    v.drain()
                    inc(v.engine_nop(), s_V)
                    setmark(f"p2negl_{gr}", s_V)
                    wait(v, f"p2ge_{gr}")
                    v.tensor_scalar(g8[:, 0:128], g8[:, 128:256], 1.0, 1e12,
                                    ALU.add, ALU.min)
                    v.drain()
                    v.reciprocal(g8[:, 128:256], g8[:, 0:128])
                    v.drain()
                    inc(v.engine_nop(), s_V)
                    setmark(f"p2g8_{gr}", s_V)
                    wait(v, f"p2gt_{gr}")
                    gsl = q_pre[0][:, 3 * HT:3 * HT + 8]
                    v.tensor_copy(gtm[:, 5, :], gsl)
                    v.drain()
                    v.max(gtm[:, 0, :], gtm[:, 5, :])
                    v.drain()
                    v.max_index(gtmi[:, :], gtm[:, 0, :], gtm[:, 5, :])
                    v.drain()
                    v.tensor_copy(gtm[:, 1, :], gtmi[:, :])
                    v.drain()
                    v.tensor_scalar(gtm[:, 2, :], iota8[:, :], gtm[:, 1, 0:1], None,
                                    ALU.is_equal)
                    v.tensor_scalar(gtm[:, 3, :], iota8[:, :], gtm[:, 1, 1:2], None,
                                    ALU.is_equal)
                    v.drain()
                    v.tensor_add(gtm[:, 2, :], gtm[:, 2, :], gtm[:, 3, :])
                    v.tensor_add(gtm[:, 4, 0:1], gtm[:, 0, 0:1], gtm[:, 0, 1:2])
                    v.drain()
                    v.tensor_mul(gtm[:, 3, :], gtm[:, 5, :], gtm[:, 2, :])
                    v.reciprocal(gtm[:, 4, 1:2], gtm[:, 4, 0:1])
                    v.drain()
                    vv = v.tensor_scalar(gn[:, gr, :], gtm[:, 3, :], gtm[:, 4, 1:2],
                                         None, ALU.mult)
                    inc(vv, s_V)
                    setmark(f"p2gn_{gr}", s_V)
                    wait(v, f"p2gnT_{gr}")
                    if gr >= 1:
                        wait(v, f"p2b2_{gr-1}")
                    inc(v.tensor_copy(gnT[:, :], q_pre[0][0:8, HT:2 * HT]), s_V)
                    setmark(f"p2gnTv_{gr}", s_V)
                    if g_i >= 1 and gg == 0:
                        wait(v, f"ystore_{g_i-1}")
                    wait(v, f"p2b2_{gr}")
                    for dm in range(8):
                        wait(v, f"p2tr_{gr}_{dm}")
                        v.tensor_copy(acc[:, gg, dm * 128:(dm + 1) * 128],
                                      q_pre[dm % 2][:, 2 * HT:3 * HT])
                        v.drain()
                        vv = v.tensor_add(acc[:, gg, dm * 128:(dm + 1) * 128],
                                          acc[:, gg, dm * 128:(dm + 1) * 128],
                                          q_mm2[0][:, dm * 128:(dm + 1) * 128])
                        inc(vv, s_V)
                        setmark(f"p2ai_{gr}_{dm}", s_V)
                        v.drain()
                    setmark(f"accinit_{g_i}_{gg}", s_V)
                for e in range(E):
                    for gg in range(8):
                        gr = g_i * 8 + gg
                        if gr >= n_tiles:
                            continue
                        wait(v, f"mm2_{g_i}_{e}_{gg}")
                        vv = v.scalar_tensor_tensor(acc[:, gg, :], q_mm2[gg % 2][:, 0:1024],
                                                    gn[:, gr, e:e + 1], acc[:, gg, :],
                                                    ALU.mult, ALU.add)
                        inc(vv, s_V)
                        setmark(f"accev_{g_i}_{e}_{gg}", s_V)
                        v.drain()
                setmark(f"accdone_{g_i}", s_V)

        if real:
            with nc.Block() as block:
                @block.sync
                def _(sy):
                    sec_sync(sy)

                @block.gpsimd
                def _(gp):
                    sec_gp(gp)

                @block.tensor
                def _(t):
                    sec_tensor(t)

                @block.scalar
                def _(s):
                    sec_scalar(s)

                @block.vector
                def _(v):
                    sec_vector(v)
        else:
            fake = _FakeEng()
            sec_sync(fake)
            sec_gp(fake)
            sec_tensor(fake)
            sec_scalar(fake)
            sec_vector(fake)

    nc._dbg = dict(u_bf=u_bf, gn=gn, h2=h2, acc=acc, xag=xag, w1=w1, w2=w2,
                   gnT=gnT, g8=g8, b2sb=b2, centT=centT, b1sb=b1)
    run_pass(False)
    totals1 = {s.name: s.n for s in all_sems}
    marks1 = dict(marks)
    run_pass(True)
    totals2 = {s.name: s.n for s in all_sems}
    for k_ in totals1:
        assert totals1[k_] == totals2[k_], (
            f"pass mismatch sem {k_}: pass1={totals1[k_]} pass2={totals2[k_]}")
    for k_, v_ in marks1.items():
        assert marks[k_] == v_, f"mark mismatch {k_}: pass1={v_} pass2={marks[k_]}"
    return nc



def _prep_core_inputs(c, x, norm_w, in_proj_w, conv_w, conv_b, x_proj_w,
                      dt_proj_w, dt_proj_b, A_log, Dp, out_proj_w,
                      centroids, W1, b1, W2, b2, n_tiles=16):
    b, j = c // 2, c % 2
    f32 = np.float32
    sl = slice(j * 1024, (j + 1) * 1024)

    xb = np.ascontiguousarray(x[b].T).astype(f32)
    x_perm = np.ascontiguousarray(
        xb.reshape(8, 128, n_tiles, TT).transpose(1, 2, 0, 3))

    ipT = in_proj_w.T
    w_in = np.concatenate(
        [ipT[:, sl], ipT[:, 2048 + j * 1024:2048 + (j + 1) * 1024]], 1)
    w_in = np.ascontiguousarray(
        w_in.reshape(8, 128, 2048).transpose(1, 0, 2)).astype(f32)

    xpT = np.ascontiguousarray(
        x_proj_w.T[sl].reshape(8, 128, 96).transpose(1, 0, 2)).astype(f32)
    dtT = np.ascontiguousarray(dt_proj_w.T[:, sl].reshape(64, 8, 128)).astype(f32)
    opT = np.ascontiguousarray(
        out_proj_w.T[sl].reshape(8, 128, 1024).transpose(1, 0, 2)).astype(f32)

    def vec8(v):
        return np.ascontiguousarray(v[sl].reshape(8, 128).T).astype(f32)

    convw_h = np.ascontiguousarray(
        conv_w[sl].reshape(8, 128, 4).transpose(1, 0, 2)).astype(f32)
    alog_h = np.ascontiguousarray(
        A_log[sl].reshape(8, 128, 16).transpose(1, 0, 2)).astype(f32)
    normw_h = np.ascontiguousarray(norm_w.reshape(8, 128).T).astype(f32)
    cT = np.ascontiguousarray(
        centroids.T.reshape(8, 128, 8).transpose(1, 0, 2)).astype(f32)
    w1h = np.ascontiguousarray(
        W1.reshape(8, 8, 128, 4096).transpose(0, 2, 1, 3)).astype(f32)
    w2h = np.ascontiguousarray(
        W2.reshape(8, 32, 128, 1024).transpose(0, 2, 1, 3)).astype(f32)
    b1h = np.ascontiguousarray(b1.reshape(8, 32, 128).transpose(2, 0, 1)).astype(f32)

    oh16 = np.zeros((16, 16, 128), dtype=np.float32)
    for n in range(16):
        oh16[n, n, :] = 1.0
    oh16 = np.ascontiguousarray(oh16.transpose(1, 0, 2)).astype(ml_dtypes.bfloat16)

    return {
        "x_in": x_perm,
        "w_in": w_in,
        "xprojT": xpT,
        "dtprojT": dtT,
        "outprojT": opT,
        "convw": convw_h,
        "convb": vec8(conv_b),
        "dtb": vec8(dt_proj_b),
        "alog": alog_h,
        "dp_in": vec8(Dp),
        "normw": normw_h,
        "centT": cT,
        "w1_in": w1h,
        "w2_in": w2h,
        "b1_in": b1h,
        "b2_in": np.ascontiguousarray(b2).astype(f32),
        "onesc": np.ones((128, 1), dtype=f32),
        "onesr": np.ones((1, 128), dtype=f32),
        "onesrb": np.ones((1, 128), dtype=ml_dtypes.bfloat16),
        "oh_in": oh16,
        "id_in": np.eye(128, dtype=f32),
        "iota8": np.broadcast_to(np.arange(8, dtype=f32), (128, 8)).copy(),
    }


_NC_CACHE = {}


def kernel(L0_attn_x, L0_attn_norm_w, L0_attn_in_proj_w, L0_attn_conv_w,
           L0_attn_conv_b, L0_attn_x_proj_w, L0_attn_dt_proj_w,
           L0_attn_dt_proj_b, L0_attn_A_log, L0_attn_Dp, L0_attn_out_proj_w,
           L0_ffn_centroids, L0_ffn_W1, L0_ffn_b1, L0_ffn_W2, L0_ffn_b2):
    n_tiles = 16
    key = ("full", n_tiles)
    if key not in _NC_CACHE:
        _NC_CACHE[key] = build_nc(n_tiles=n_tiles)
    nc = _NC_CACHE[key]

    args = (np.asarray(L0_attn_x, np.float32), np.asarray(L0_attn_norm_w, np.float32),
            np.asarray(L0_attn_in_proj_w, np.float32),
            np.asarray(L0_attn_conv_w, np.float32),
            np.asarray(L0_attn_conv_b, np.float32),
            np.asarray(L0_attn_x_proj_w, np.float32),
            np.asarray(L0_attn_dt_proj_w, np.float32),
            np.asarray(L0_attn_dt_proj_b, np.float32),
            np.asarray(L0_attn_A_log, np.float32), np.asarray(L0_attn_Dp, np.float32),
            np.asarray(L0_attn_out_proj_w, np.float32),
            np.asarray(L0_ffn_centroids, np.float32),
            np.asarray(L0_ffn_W1, np.float32), np.asarray(L0_ffn_b1, np.float32),
            np.asarray(L0_ffn_W2, np.float32), np.asarray(L0_ffn_b2, np.float32))

    in_maps = [_prep_core_inputs(c, *args, n_tiles=n_tiles) for c in range(8)]
    res = run_bass_kernel_spmd(nc, in_maps, core_ids=list(range(8)))

    out = np.empty((B_, L_, D_), dtype=np.float32)
    for c in range(8):
        b, j = c // 2, c % 2
        ym = res.results[c]["y_my"].reshape(n_tiles, HT, 1024)
        out.reshape(B_, n_tiles, 2, HT, D_)[b, :, j, :, :] = ym
    return out


# version 07
# speedup vs baseline: 1.1891x; 1.1891x over previous
import numpy as np
import ml_dtypes
import concourse.bass as bass
import concourse.mybir as mybir
from concourse.bass_utils import run_bass_kernel_spmd
from contextlib import ExitStack

F32 = mybir.dt.float32
BF16 = mybir.dt.bfloat16
U32 = mybir.dt.uint32
AF = mybir.ActivationFunctionType
ALU = mybir.AluOpType
AX = mybir.AxisListType

B_, L_, D_ = 4, 4096, 1024
TT = 256
HT = TT // 2
E = 8
EPS = 1e-5
PAIRS = [[0, 1], [2, 3], [4, 5], [6, 7]]


class _Sem:
    __slots__ = ("h", "name", "n")

    def __init__(self, handle, name):
        self.h, self.name, self.n = handle, name, 0


class _FakeInst:
    def then_inc(self, *a, **k):
        return self


class _FakeReg:
    def __enter__(self):
        return self

    def __exit__(self, *a):
        return False


class _FakeEng:
    def register(self, *a, **k):
        return _FakeReg()

    def __getattr__(self, name):
        def f(*a, **k):
            return _FakeInst()
        return f


def build_nc(n_tiles=16, dbg=False):
    nc = bass.Bass("TRN2", target_bir_lowering=False)
    NTG = (n_tiles + 7) // 8

    D = nc.dram_tensor
    x_in = D("x_in", [128, n_tiles, 8, TT], F32, kind="ExternalInput")
    w_in_d = D("w_in", [128, 8, 2048], F32, kind="ExternalInput")
    xprojT_d = D("xprojT", [128, 8, 96], F32, kind="ExternalInput")
    dtprojT_d = D("dtprojT", [64, 8, 128], F32, kind="ExternalInput")
    outprojT_d = D("outprojT", [128, 8, 1024], F32, kind="ExternalInput")
    convw_d = D("convw", [128, 8, 4], F32, kind="ExternalInput")
    convb_d = D("convb", [128, 8], F32, kind="ExternalInput")
    dtb_d = D("dtb", [128, 8], F32, kind="ExternalInput")
    alog_d = D("alog", [128, 8, 16], F32, kind="ExternalInput")
    dp_d = D("dp_in", [128, 8], F32, kind="ExternalInput")
    normw_d = D("normw", [128, 8], F32, kind="ExternalInput")
    centT_d = D("centT", [128, 8, 8], F32, kind="ExternalInput")
    w1_d = D("w1_in", [8, 128, 8, 4096], F32, kind="ExternalInput")
    w2_d = D("w2_in", [8, 128, 32, 1024], F32, kind="ExternalInput")
    b1_d = D("b1_in", [128, 8, 32], F32, kind="ExternalInput")
    b2_d = D("b2_in", [8, 1024], F32, kind="ExternalInput")
    onesc_d = D("onesc", [128, 1], F32, kind="ExternalInput")
    onesr_d = D("onesr", [1, 128], F32, kind="ExternalInput")
    onesrb_d = D("onesrb", [1, 128], BF16, kind="ExternalInput")
    oh_d = D("oh_in", [16, 16, 128], BF16, kind="ExternalInput")
    id_d = D("id_in", [128, 128], F32, kind="ExternalInput")
    idb_d = D("idb_in", [128, 128], BF16, kind="ExternalInput")
    iota8_d = D("iota8", [128, 8], F32, kind="ExternalInput")

    y_my = D("y_my", [n_tiles * HT, 1024], F32, kind="ExternalOutput")
    if dbg:
        xattn_dbg = D("xattn_dbg", [128, 8, n_tiles, HT], F32, kind="ExternalOutput")

    rps_dram = D("rps_dram", [n_tiles, 128, 8, HT], F32, kind="Internal")
    ops_dram = D("ops_dram", [n_tiles, 128, 8, HT], F32, kind="Internal")
    dbl_ci = [D(f"dbl_ci{i}", [96, TT], F32, kind="Internal") for i in range(2)]
    dbl_co = [D(f"dbl_co{i}", [96, TT], F32, kind="Internal") for i in range(2)]
    rp_ci = [D(f"rp_ci{i}", [2, 128, 8, HT], F32, kind="Internal") for i in range(2)]
    op_ci = [D(f"op_ci{i}", [2, 128, 8, HT], F32, kind="Internal") for i in range(2)]

    es = ExitStack()

    def sem(name):
        return _Sem(es.enter_context(nc.semaphore(name)), name)

    s_pl = sem("s_pl")
    s_x = [sem("s_x0"), sem("s_x1")]
    s_db = sem("s_db")
    s_rp = sem("s_rp")
    s_op = sem("s_op")
    s_y = sem("s_y")
    s_dbg = sem("s_dbg")
    s_wp = sem("s_wp")
    s_dl = sem("s_dl")
    s_w1 = sem("s_w1")
    s_w1b = sem("s_w1b")
    s_w2 = sem("s_w2")
    s_w2b = sem("s_w2b")
    s_xg = sem("s_xg")
    s_cd = sem("s_cd")
    s_co = sem("s_co")
    s_T = sem("s_T")
    s_S = sem("s_S")
    s_V = sem("s_V")
    s_G = sem("s_G")
    all_sems = [s_pl, s_x[0], s_x[1], s_db, s_rp, s_op, s_y, s_dbg, s_wp,
                s_dl, s_w1, s_w1b, s_w2, s_w2b, s_xg, s_cd, s_co, s_T, s_S, s_V, s_G]

    A = nc.alloc_sbuf_tensor
    onesc = A("onesc_s", [128, 1], F32)
    onesr = A("onesr_s", [1, 128], F32)
    onesrb = A("onesrb_s", [1, 128], BF16)
    idm = A("idm_s", [128, 128], F32)
    idb = A("idb_s", [128, 128], BF16)
    pid = A("pid_s", [1, 1], U32)
    statrow = A("statrow_s", [1, 512], F32)
    tiny = A("tiny_s", [1, 512], F32)
    epsc = A("epsc_s", [1, 1], F32)
    iota8 = A("iota8_s", [128, 8], F32)

    p1g = []

    def A1(name, shape, dt):
        g = nc.sbuf_tensor(name, shape, dt)
        t = g.__enter__()
        p1g.append(g)
        return t

    winT = A1("winT", [128, 8, 2048], BF16)
    xprojT = A1("xprojT_s", [128, 8, 96], BF16)
    dtprojT = A1("dtprojT_s", [64, 8, 128], BF16)
    outprojT = A1("outprojT_s", [128, 8, 1024], BF16)
    convw = A1("convw_s", [128, 8, 4], F32)
    convb = A1("convb_s", [128, 8], F32)
    dtb = A1("dtb_s", [128, 8], F32)
    aneg = A1("aneg_s", [128, 8, 16], F32)
    dp = A1("dp_s", [128, 8], F32)
    normw = A1("normw_s", [128, 8], F32)
    oh = A1("oh_s", [16, 16, 128], BF16)
    hstate = A1("hstate_s", [128, 8, 16], F32)
    x_sb = [A1(f"x_sb{i}", [128, 8, TT], F32) for i in range(2)]
    xcbf = [A1(f"xcbf{i}", [128, 8, TT], BF16) for i in range(2)]
    szbf = [A1(f"szbf{i}", [128, 8, TT], BF16) for i in range(2)]
    deltab = [A1(f"deltab{i}", [128, 8, TT], BF16) for i in range(2)]
    wbf = [A1(f"wbf{i}", [128, 8, TT], BF16) for i in range(2)]
    brep = A1("brep_b", [128, 16, TT], BF16)
    crep = A1("crep_b", [128, 16, TT], BF16)
    dtr = A1("dtr_b", [64, TT], BF16)
    braw = A1("braw_b", [16, TT], BF16)
    craw = A1("craw_b", [16, TT], BF16)
    xn = A1("xn_b", [128, 8, TT], F32)
    xn2bf = A1("xn2bf_b", [128, 8, TT], BF16)
    rpre = A1("rpre_b", [128, 8, TT], F32)
    xihalo = A1("xihalo_b", [128, 8, TT + 3], BF16)
    ctmp = [A1(f"ctmp{i}", [128, TT], F32) for i in range(2)]
    sp1 = A1("sp1_b", [128, TT], F32)
    da2 = [A1(f"da2_{i}", [128, 8, TT], F32) for i in range(2)]
    h_b = A1("h_b", [128, 8, TT], BF16)
    ybf = A1("ybf_b", [128, 8, TT], BF16)
    opart = A1("opart_b", [128, 8, 2, HT], F32)
    dbub = [A1(f"dbub{i}", [128, 8 * TT], BF16) for i in range(2)]
    dpd = A1("dpd_b", [128, 8, 128], BF16)
    convo = xn
    dbu2 = dbub


    ppg = [nc.psum_tensor(f"pp{i}", [128, 512], F32) for i in range(8)]
    pp = [g.__enter__() for g in ppg]
    for g in reversed(ppg):
        g.__exit__(None, None, None)
    for g in reversed(p1g):
        g.__exit__(None, None, None)

    p2g = []

    def A2(name, shape, dt):
        g = nc.sbuf_tensor(name, shape, dt)
        t = g.__enter__()
        p2g.append(g)
        return t

    w1 = A2("w1_b", [128, 8, 4096], BF16)
    w2 = A2("w2_b", [128, 32, 1024], BF16)
    u_bf = A2("u_bf_b", [128, 8, 1024], BF16)
    acc = A2("acc_b", [128, 8, 1024], F32)
    h2 = A2("h2_b", [128, 32, 256], BF16)
    xag = A2("xag_b", [128, 8, 128], F32)
    centT = A2("centT_s", [128, 8, 8], F32)
    b1 = A2("b1_s", [128, 8, 32], F32)
    b2 = A2("b2_s", [8, 1024], BF16)
    gn = A2("gn_s", [128, n_tiles, 8], F32)
    gnT = A2("gnT_s", [8, 128], BF16)
    g8 = A2("g8_s", [8, 256], F32)
    gtm = A2("gtm_s", [128, 6, 8], F32)
    gtmi = A2("gtmi_s", [128, 8], U32)

    qg = ([nc.psum_tensor(f"qa{i}", [128, 512], F32) for i in range(2)]
          + [nc.psum_tensor(f"qb{i}", [128, 1024], F32) for i in range(2)]
          + [nc.psum_tensor(f"qc{i}", [128, 512], F32) for i in range(2)])
    qs = [g.__enter__() for g in qg]
    q_mm1, q_mm2, q_pre = qs[0:2], qs[2:4], qs[4:6]
    for g in reversed(qg):
        g.__exit__(None, None, None)
    for g in reversed(p2g):
        g.__exit__(None, None, None)

    marks = {}

    def run_pass(real):
        for s in all_sems:
            s.n = 0
        bank_last = {}
        YB = (2, 5, 6, 7)

        def inc(inst, s, k=1):
            s.n += k
            inst.then_inc(s.h, k)
            return inst

        def setmark(name, s):
            marks[name] = (s.h, s.n)

        def wait(eng, name):
            if real:
                h, v = marks[name]
                eng.wait_ge(h, v)

        def tuse(t, bid, consumer_mark):
            if bid in bank_last:
                wait(t, bank_last[bid])
            bank_last[bid] = consumer_mark

        def sec_sync(sy):
            for t_sb, t_dr in [
                (convw, convw_d), (convb, convb_d), (dtb, dtb_d),
                (aneg, alog_d), (dp, dp_d), (normw, normw_d),
                (onesc, onesc_d), (onesr, onesr_d), (onesrb, onesrb_d),
                (idm, id_d), (idb, idb_d), (iota8, iota8_d), (pid, nc.partition_id_tensor),
            ]:
                inc(sy.dma_start(out=t_sb.ap(), in_=t_dr.ap()), s_pl, 16)
            setmark("prolog_ld", s_pl)

            for k in range(n_tiles):
                p = k % 2
                if k >= 2:
                    wait(sy, f"rprev_{k-2}")
                inc(sy.dma_start(out=x_sb[p].ap(), in_=x_in[:, k, :, :]), s_x[p], 16)
                setmark(f"xload_{k}", s_x[p])

                if k >= 2:
                    kk = k - 2
                    wait(sy, f"opartv_{kk}")
                    if kk >= 2:
                        wait(sy, f"opcc_{kk-2}")
                    inc(sy.dma_start(out=op_ci[kk % 2][0], in_=opart[:, :, 0, :]), s_op, 16)
                    inc(sy.dma_start(out=op_ci[kk % 2][1], in_=opart[:, :, 1, :]), s_op, 16)
                    setmark(f"opstore_{kk}", s_op)

                wait(sy, f"rprev_{k}")
                if k >= 2:
                    wait(sy, f"rpcc_{k-2}")
                inc(sy.dma_start(out=rp_ci[p][0], in_=rpre[:, :, 0:HT]), s_rp, 16)
                inc(sy.dma_start(out=rp_ci[p][1], in_=rpre[:, :, HT:TT]), s_rp, 16)
                setmark(f"rprestore_{k}", s_rp)

                wait(sy, f"dblmy_{k}")
                if k >= 2:
                    wait(sy, f"dblcc_{k-2}")
                inc(sy.dma_start(out=dbl_ci[p].ap(), in_=ctmp[0][0:96, :]), s_db, 16)
                setmark(f"dblstore_{k}", s_db)

            for kk in (n_tiles - 2, n_tiles - 1):
                if kk >= 0:
                    wait(sy, f"opartv_{kk}")
                    if kk >= 2:
                        wait(sy, f"opcc_{kk-2}")
                    inc(sy.dma_start(out=op_ci[kk % 2][0], in_=opart[:, :, 0, :]), s_op, 16)
                    inc(sy.dma_start(out=op_ci[kk % 2][1], in_=opart[:, :, 1, :]), s_op, 16)
                    setmark(f"opstore_{kk}", s_op)
            wait(sy, "p1v")
            wait(sy, f"op_{n_tiles-1}_{7}")
            inc(sy.dma_start(out=centT.ap(), in_=centT_d.ap()), s_pl, 16)
            inc(sy.dma_start(out=b1.ap(), in_=b1_d.ap()), s_pl, 16)
            setmark("p2consts", s_pl)
            for g_i in range(NTG):
                for gg in range(8):
                    gr = g_i * 8 + gg
                    if gr >= n_tiles:
                        continue
                    if dbg:
                        wait(sy, f"xagv_{gr}")
                        inc(sy.dma_start(out=xattn_dbg[:, :, gr, :], in_=xag.ap()),
                            s_dbg, 16)
                        setmark(f"xadbg_{gr}", s_dbg)
                wait(sy, f"accdone_{g_i}")
                for gg in range(8):
                    gr = g_i * 8 + gg
                    if gr >= n_tiles:
                        continue
                    inc(sy.dma_start(out=y_my[gr * HT:(gr + 1) * HT, :],
                                     in_=acc[:, gg, :]), s_y, 16)
                setmark(f"ystore_{g_i}", s_y)
            if real:
                for s_ in (s_db, s_rp, s_op, s_y, s_dbg):
                    if s_.n:
                        sy.wait_ge(s_.h, s_.n)

        def sec_gp(gp):
            for t_sb, t_dr in [(winT, w_in_d), (xprojT, xprojT_d),
                               (outprojT, outprojT_d), (dtprojT, dtprojT_d),
                               (oh, oh_d)]:
                inc(gp.dma_start(out=t_sb.ap(), in_=t_dr.ap()), s_wp, 16)
            setmark("prolog_w", s_wp)

            for k in range(n_tiles):
                p = k % 2
                if k >= 2:
                    kk = k - 2
                    wait(gp, f"opstore_{kk}")
                    inc(gp.collective_compute(
                        "ReduceScatter", ALU.add, replica_groups=PAIRS,
                        ins=[op_ci[kk % 2].ap().opt()],
                        outs=[ops_dram[kk].opt()]), s_co, 1)
                    setmark(f"opcc_{kk}", s_co)
                def gp_dbu(kk1, n0, n1):
                    for n in range(n0, n1):
                        wait(gp, f"wk_{kk1}")
                        wait(gp, f"bbev_{kk1}_{n}")
                        if n >= 2:
                            wait(gp, f"ym_{kk1}_{n-2}")
                        for c in range(8):
                            vv = gp.tensor_mul(dbu2[n % 2][:, c * TT:(c + 1) * TT],
                                               wbf[kk1 % 2][:, c, :], brep[:, n, :])
                        gp.drain()
                        inc(vv, s_G)
                        setmark(f"dbu_{kk1}_{n}", s_G)
                if k >= 1:
                    gp_dbu(k - 1, 0, 8)
                wait(gp, f"rprestore_{k}")
                inc(gp.collective_compute(
                    "ReduceScatter", ALU.add, replica_groups=PAIRS,
                    ins=[rp_ci[p].ap().opt()], outs=[rps_dram[k].opt()]), s_cd, 1)
                setmark(f"rpcc_{k}", s_cd)
                wait(gp, f"dblstore_{k}")
                inc(gp.collective_compute(
                    "AllReduce", ALU.add, replica_groups=PAIRS,
                    ins=[dbl_ci[p].ap().opt()], outs=[dbl_co[p].ap().opt()]), s_cd, 1)
                setmark(f"dblcc_{k}", s_cd)
                wait(gp, f"dblcc_{k}")
                if k >= 2:
                    wait(gp, f"scandone_{k-2}")
                inc(gp.dma_start(out=dtr.ap(), in_=dbl_co[p][0:64, :]), s_dl, 16)
                inc(gp.dma_start(out=braw.ap(), in_=dbl_co[p][64:80, :]), s_dl, 16)
                inc(gp.dma_start(out=craw.ap(), in_=dbl_co[p][80:96, :]), s_dl, 16)
                setmark(f"dblld_{k}", s_dl)
                if k >= 1:
                    gp_dbu(k - 1, 8, 16)
            kk1 = n_tiles - 1
            for n in range(16):
                wait(gp, f"wk_{kk1}")
                wait(gp, f"bbev_{kk1}_{n}")
                if n >= 2:
                    wait(gp, f"ym_{kk1}_{n-2}")
                for c in range(8):
                    vv = gp.tensor_mul(dbu2[n % 2][:, c * TT:(c + 1) * TT],
                                       wbf[kk1 % 2][:, c, :], brep[:, n, :])
                gp.drain()
                inc(vv, s_G)
                setmark(f"dbu_{kk1}_{n}", s_G)
            del gp_dbu

            for kk in (n_tiles - 2, n_tiles - 1):
                if kk >= 0:
                    wait(gp, f"opstore_{kk}")
                    inc(gp.collective_compute(
                        "ReduceScatter", ALU.add, replica_groups=PAIRS,
                        ins=[op_ci[kk % 2].ap().opt()],
                        outs=[ops_dram[kk].opt()]), s_co, 1)
                    setmark(f"opcc_{kk}", s_co)
            wait(gp, "p1v")
            wait(gp, f"rprestore_{n_tiles-1}")
            wait(gp, f"opstore_{n_tiles-1}")
            inc(gp.dma_start(out=b2.ap(), in_=b2_d.ap()), s_wp, 16)
            setmark("p2b2ld", s_wp)
            for g_i in range(NTG):
                for gg in range(8):
                    gr = g_i * 8 + gg
                    if gr >= n_tiles:
                        continue
                    wait(gp, f"opcc_{gr}")
                    if gr >= 1:
                        wait(gp, f"xagT_{gr-1}")
                    if dbg and gr >= 1:
                        wait(gp, f"xadbg_{gr-1}")
                    wait(gp, f"rpcc_{gr}")
                    inc(gp.dma_start(out=xag.ap(), in_=rps_dram[gr]), s_xg, 16)
                    if real:
                        gp.wait_ge(s_xg.h, s_xg.n)
                    inc(gp.dma_start(out=xag.ap(), in_=ops_dram[gr],
                                     accum_op=ALU.add), s_xg, 16)
                    setmark(f"xagv_{gr}", s_xg)
                for e in range(E):
                    first = (g_i == 0 and e == 0)
                    prev = f"{g_i}_{e-1}" if e > 0 else f"{g_i-1}_{E-1}"
                    if first:
                        pass
                    else:
                        wait(gp, f"mm1h_{prev}")
                    inc(gp.dma_start(out=w1[:, :, 0:2048], in_=w1_d[e, :, :, 0:2048]),
                        s_w1, 16)
                    setmark(f"w1a_{g_i}_{e}", s_w1)
                    if not first:
                        wait(gp, f"mm1d_{prev}")
                    inc(gp.dma_start(out=w1[:, :, 2048:4096], in_=w1_d[e, :, :, 2048:4096]),
                        s_w1b, 16)
                    setmark(f"w1_{g_i}_{e}", s_w1b)
                    if not first:
                        wait(gp, f"mm2d_{prev}")
                    inc(gp.dma_start(out=w2[:, :, 0:512], in_=w2_d[e, :, :, 0:512]),
                        s_w2, 16)
                    setmark(f"w2_{g_i}_{e}", s_w2)
                    inc(gp.dma_start(out=w2[:, :, 512:1024], in_=w2_d[e, :, :, 512:1024]),
                        s_w2b, 16)
                    setmark(f"w2b_{g_i}_{e}", s_w2b)

        def t_front(t, k):
            p = k % 2
            wait(t, f"xload_{k}")
            tuse(t, 0, f"stA_{k}")
            for c in range(8):
                mm = t.matmul(pp[0][0:1, 0:TT], onesc[:, 0:1], x_sb[p][:, c, :],
                              start=(c == 0), stop=(c == 7))
            inc(mm, s_T)
            setmark(f"sumx_{k}", s_T)
            wait(t, f"xsq_{k}")
            for c in range(8):
                mm = t.matmul(pp[0][0:1, TT:2 * TT], onesc[:, 0:1], xn[:, c, :],
                              start=(c == 0), stop=(c == 7))
            inc(mm, s_T)
            setmark(f"sumx2_{k}", s_T)
            wait(t, f"statrow_{k}")
            tuse(t, 1, f"lnapply_{k}")
            inc(t.matmul(pp[1][:, 0:512], onesr[0:1, :], statrow[0:1, 0:512],
                         start=True, stop=True), s_T)
            setmark(f"bcast1_{k}", s_T)
            wait(t, f"xnsq_{k}")
            for c in range(8):
                mm = t.matmul(pp[0][0:1, 0:TT], onesc[:, 0:1], rpre[:, c, :],
                              start=(c == 0), stop=(c == 7))
            inc(mm, s_T)
            setmark(f"sumxn2_{k}", s_T)
            wait(t, f"rrmsrow_{k}")
            tuse(t, 0, f"xn2v_{k}")
            inc(t.matmul(pp[0][:, TT:2 * TT], onesr[0:1, :], statrow[0:1, 256:256 + TT],
                         start=True, stop=True), s_T)
            setmark(f"bcast2_{k}", s_T)

        def t_front_ip(t, k):
            p = k % 2
            wait(t, f"xn2v_{k}")
            for f in range(16):
                bid = 3 + (f % 2)
                tuse(t, bid, f"ipev_{k}_{f}")
                for c in range(8):
                    mm = t.matmul(pp[bid][:, 0:TT], winT[:, c, f * 128:(f + 1) * 128],
                                  xn2bf[:, c, :], start=(c == 0), stop=(c == 7))
                inc(mm, s_T)
                setmark(f"ip_{k}_{f}", s_T)

        def t_tail_prev(t, k):
            p = k % 2
            wait(t, f"dblld_{k}")
            for c in range(8):
                bid = 3 + (c % 2)
                tuse(t, bid, f"dtev_{k}_{c}")
                inc(t.matmul(pp[bid][:, 0:TT], dtprojT[:, c, :], dtr[:, :],
                             start=True, stop=True), s_T)
                setmark(f"dt_{k}_{c}", s_T)
            for n in range(16):
                bid = 3 + (n % 2)
                tuse(t, bid, f"bbev_{k}_{n}")
                inc(t.matmul(pp[bid][:, 0:TT], oh[:, n, :], braw[:, :],
                             start=True, stop=True), s_T)
                setmark(f"bb_{k}_{n}", s_T)
            for n in range(16):
                bid = 3 + (n % 2)
                tuse(t, bid, f"ccev_{k}_{n}")
                inc(t.matmul(pp[bid][:, 0:TT], oh[:, n, :], craw[:, :],
                             start=True, stop=True), s_T)
                setmark(f"cb_{k}_{n}", s_T)

        def t_dbl(t, k):
            p = k % 2
            wait(t, f"xcbfs_{k}")
            tuse(t, 3, f"dblmy_{k}")
            for c in range(8):
                mm = t.matmul(pp[3][0:96, 0:TT], xprojT[:, c, :], xcbf[p][:, c, :],
                              start=(c == 0), stop=(c == 7))
            inc(mm, s_T)
            setmark(f"dblps_{k}", s_T)

        def t_ymm_init(t, k):
            p = k % 2
            if k >= 1:
                wait(t, f"ydone_{k-1}")
            else:
                wait(t, "dpdv")
            wait(t, f"xcbfs_{k}")
            for g in range(4):
                for cc in range(2):
                    c = 2 * g + cc
                    t.matmul(pp[YB[g]][:, cc * TT:(cc + 1) * TT], dpd[:, c, :],
                             xcbf[p][:, c, :], start=(cc == 0), stop=False)

        def t_ymm(t, k, n0, n1):
            for n in range(n0, n1):
                pn = n % 2
                wait(t, f"scn_{k}_{n}")
                for g in range(4):
                    mm = t.matmul(pp[YB[g]][:, 0:512], idb[:, :],
                                  dbu2[pn][:, g * 512:(g + 1) * 512],
                                  start=False, stop=(n == 15))
                inc(mm, s_T)
                setmark(f"ym_{k}_{n}", s_T)

        def t_opart(t, k):
            wait(t, f"ydone_{k}")
            for dm in range(8):
                bid = 3 + (dm % 2)
                tuse(t, bid, f"opev_{k}_{dm}")
                for c in range(8):
                    mm = t.matmul(pp[bid][:, 0:TT], outprojT[:, c, dm * 128:(dm + 1) * 128],
                                  ybf[:, c, :], start=(c == 0), stop=(c == 7))
                inc(mm, s_T)
                setmark(f"op_{k}_{dm}", s_T)

        def t_p2_pre(t, gr):
            wait(t, f"xagv_{gr}")
            if gr == 0:
                wait(t, "p1v")
            tuse(t, 10, f"p2gnTv_{gr}")
            for c in range(8):
                mm = t.matmul(q_pre[0][0:1, 0:HT], onesc[:, 0:1], xag[:, c, :],
                              start=(c == 0), stop=(c == 7))
            inc(mm, s_T)
            setmark(f"p2sumx_{gr}", s_T)
            wait(t, f"p2xsq_{gr}")
            for c in range(8):
                mm = t.matmul(q_pre[0][0:1, HT:2 * HT], onesc[:, 0:1],
                              acc[:, gr % 8, c * 128:(c + 1) * 128],
                              start=(c == 0), stop=(c == 7))
            inc(mm, s_T)
            setmark(f"p2sumx2_{gr}", s_T)
            wait(t, f"p2statrow_{gr}")
            tuse(t, 11, f"p2lnapply_{gr}")
            inc(t.matmul(q_pre[1][:, 0:2 * HT], onesr[0:1, :], statrow[0:1, 0:2 * HT],
                         start=True, stop=True), s_T)
            setmark(f"p2bcast_{gr}", s_T)
            wait(t, f"p2lnapply_{gr}")
            if gr == 0:
                wait(t, "p2consts")
            for c in range(8):
                mm = t.matmul(q_pre[0][0:8, 2 * HT:3 * HT], centT[:, c, :],
                              acc[:, gr % 8, c * 128:(c + 1) * 128],
                              start=(c == 0), stop=(c == 7))
            inc(mm, s_T)
            setmark(f"p2glog_{gr}", s_T)
            wait(t, f"p2g8_{gr}")
            inc(t.transpose(q_pre[0][:, 3 * HT:3 * HT + 8], g8[:, 128:256],
                            idm[0:8, 0:8]), s_T)
            setmark(f"p2gt_{gr}", s_T)
            wait(t, f"p2gn_{gr}")
            inc(t.transpose(q_pre[0][0:8, HT:2 * HT], gn[:, gr, :], idm[:, :]), s_T)
            setmark(f"p2gnT_{gr}", s_T)
            wait(t, f"p2gnTv_{gr}")
            if gr == 0:
                wait(t, "p2b2ld")
            tuse(t, 12, f"p2ai_{gr}_{7}")
            t.matmul(q_mm2[0][:, 0:512], gnT[:, :], b2[:, 0:512],
                     start=True, stop=True)
            inc(t.matmul(q_mm2[0][:, 512:1024], gnT[:, :], b2[:, 512:1024],
                         start=True, stop=True), s_T)
            setmark(f"p2b2_{gr}", s_T)
            for dm in range(8):
                bq = q_pre[dm % 2]
                if dm == 0:
                    wait(t, f"p2gnTv_{gr}")
                elif dm == 1:
                    wait(t, f"p2lnapply_{gr}")
                else:
                    wait(t, f"p2ai_{gr}_{dm-2}")
                inc(t.transpose(bq[:, 2 * HT:3 * HT], xag[:, dm, :], idm[:, :]), s_T)
                setmark(f"p2tr_{gr}_{dm}", s_T)
            setmark(f"xagT_{gr}", s_T)
            bank_last[10] = f"p2ai_{gr}_{6}"
            bank_last[11] = f"p2ai_{gr}_{7}"

        def t_p2_expert(t, g_i, e):
            gbase = g_i * 8
            tts = [tt for tt in range(4) if gbase + tt * 2 < n_tiles]
            wait(t, f"w1a_{g_i}_{e}")
            for tt in tts:
                if e == 0:
                    for g2 in range(2):
                        if gbase + tt * 2 + g2 < n_tiles:
                            wait(t, f"p2uf_{gbase + tt * 2 + g2}")
                for m in range(32):
                    bid = 14 + (m % 2)
                    if m == 16 and tt == tts[0]:
                        wait(t, f"w1_{g_i}_{e}")
                    tuse(t, bid, f"geev_{g_i}_{e}_{tt}_{m}")
                    for c in range(8):
                        mm = t.matmul(q_mm1[m % 2][:, 0:256],
                                      w1[:, c, m * 128:(m + 1) * 128],
                                      u_bf[:, c, tt * 256:(tt + 1) * 256],
                                      start=(c == 0), stop=(c == 7))
                    inc(mm, s_T)
                    setmark(f"mm1_{g_i}_{e}_{tt}_{m}", s_T)
                    if m == 15 and tt == tts[-1]:
                        setmark(f"mm1h_{g_i}_{e}", s_T)
                if tt == tts[-1]:
                    setmark(f"mm1d_{g_i}_{e}", s_T)
                wait(t, f"gedone_{g_i}_{e}_{tt}")
                if tt == tts[0]:
                    wait(t, f"w2_{g_i}_{e}")
                for g2 in range(2):
                    gg = tt * 2 + g2
                    if gbase + gg >= n_tiles:
                        continue
                    bid = 12 + g2
                    tuse(t, bid, f"accev_{g_i}_{e}_{gg}")
                    for half in range(2):
                        if half == 1 and tt == tts[0]:
                            wait(t, f"w2b_{g_i}_{e}")
                        for c in range(32):
                            mm = t.matmul(q_mm2[g2][:, half * 512:(half + 1) * 512],
                                          h2[:, c, g2 * 128:(g2 + 1) * 128],
                                          w2[:, c, half * 512:(half + 1) * 512],
                                          start=(c == 0), stop=(c == 31))
                    inc(mm, s_T)
                    setmark(f"mm2_{g_i}_{e}_{gg}", s_T)
            setmark(f"mm2d_{g_i}_{e}", s_T)

        def sec_tensor(t):
            wait(t, "prolog_ld")
            wait(t, "prolog_w")
            for k in range(n_tiles):
                if k >= 1:
                    t_tail_prev(t, k - 1)
                if k >= 2:
                    t_opart(t, k - 2)
                t_front(t, k)
                if k >= 1:
                    t_ymm_init(t, k - 1)
                    t_ymm(t, k - 1, 0, 2)
                t_front_ip(t, k)
                if k >= 1:
                    t_ymm(t, k - 1, 2, 6)
                t_dbl(t, k)
                if k >= 1:
                    t_ymm(t, k - 1, 6, 16)
            t_tail_prev(t, n_tiles - 1)
            if n_tiles >= 2:
                t_opart(t, n_tiles - 2)
            t_ymm_init(t, n_tiles - 1)
            t_ymm(t, n_tiles - 1, 0, 16)
            t_opart(t, n_tiles - 1)
            for g_i in range(NTG):
                for gg in range(8):
                    gr = g_i * 8 + gg
                    if gr < n_tiles:
                        t_p2_pre(t, gr)
                for e in range(E):
                    t_p2_expert(t, g_i, e)

        def s_front_a(s, k):
            p = k % 2
            wait(s, f"xload_{k}")
            if k == 0:
                wait(s, "anegdone")
            if k >= 1:
                wait(s, f"convt_{k-1}")
            for c in range(8):
                act = s.activation(xn[:, c, :], x_sb[p][:, c, :], AF.Square)
            inc(act, s_S)
            setmark(f"xsq_{k}", s_S)
            wait(s, f"sumx_{k}")
            wait(s, f"sumx2_{k}")
            if k >= 1:
                wait(s, f"statrow_{k-1}")
                wait(s, f"rrmsrow_{k-1}")
            s.activation(statrow[0:1, 0:TT], pp[0][0:1, 0:TT], AF.Copy, scale=1.0 / 1024)
            s.activation(tiny[0:1, 0:TT], pp[0][0:1, TT:2 * TT], AF.Copy,
                         scale=1.0 / 1024)
            s.drain()
            inc(s.activation(tiny[0:1, TT:2 * TT], statrow[0:1, 0:TT], AF.Square), s_S)
            setmark(f"stA_{k}", s_S)
            wait(s, f"stB_{k}")
            inc(s.activation(tiny[0:1, 0:TT], tiny[0:1, TT:2 * TT], AF.Sqrt, bias=epsc[0:1, 0:1]),
                s_S)
            setmark(f"stC_{k}", s_S)
            wait(s, f"lnapply_{k}")
            if k >= 1:
                wait(s, f"rprestore_{k-1}")
            for c in range(8):
                act = s.activation(rpre[:, c, :], xn[:, c, :], AF.Square)
            inc(act, s_S)
            setmark(f"xnsq_{k}", s_S)
            wait(s, f"sumxn2_{k}")
            wait(s, f"bcast1_{k}")
            s.activation(tiny[0:1, TT:2 * TT], pp[0][0:1, 0:TT], AF.Copy,
                         scale=1.0 / 1024)
            s.drain()
            inc(s.activation(tiny[0:1, 0:TT], tiny[0:1, TT:2 * TT], AF.Sqrt,
                             bias=epsc[0:1, 0:1]), s_S)
            setmark(f"rmsrow_{k}", s_S)
        def s_front_b(s, k):
            p = k % 2
            if k >= 2:
                wait(s, f"ydone_{k-2}")
            for f in range(16):
                bid = 3 + (f % 2)
                wait(s, f"ip_{k}_{f}")
                if f == 0:
                    wait(s, f"halo_{k}")
                if f < 8:
                    act = s.activation(xihalo[:, f, 3:3 + TT], pp[bid][:, 0:TT], AF.Copy)
                else:
                    act = s.activation(szbf[p][:, f - 8, :], pp[bid][:, 0:TT], AF.Silu)
                inc(act, s_S)
                setmark(f"ipev_{k}_{f}", s_S)
        def s_silu(s, k):
            p = k % 2
            wait(s, f"convt_{k}")
            for c in range(8):
                act = s.activation(xcbf[p][:, c, :], convo[:, c, :], AF.Silu,
                                   bias=convb[:, c:c + 1])
            inc(act, s_S)
            setmark(f"xcbfs_{k}", s_S)

        def s_softplus(s, k):
            p = k % 2
            if k >= 2:
                wait(s, f"wk_{k-2}")
            for c in range(8):
                bid = 3 + (c % 2)
                wait(s, f"dt_{k}_{c}")
                act = s.activation(sp1[:, :], pp[bid][:, 0:TT], AF.Exp,
                                   bias=dtb[:, c:c + 1])
                inc(act, s_S)
                setmark(f"dtev_{k}_{c}", s_S)
                s.drain()
                act = s.activation(deltab[p][:, c, :], sp1[:, :], AF.Ln, bias=1.0)
                s.drain()
            inc(act, s_S)
            setmark(f"delta_{k}", s_S)
            if k >= 1:
                wait(s, f"dbu_{k-1}_{15}")
                wait(s, f"scn_{k-1}_{15}")
            for n in range(16):
                bid = 3 + (n % 2)
                wait(s, f"bb_{k}_{n}")
                inc(s.activation(brep[:, n, :], pp[bid][:, 0:TT], AF.Copy), s_S)
                setmark(f"bbev_{k}_{n}", s_S)
            for n in range(16):
                bid = 3 + (n % 2)
                wait(s, f"cb_{k}_{n}")
                inc(s.activation(crep[:, n, :], pp[bid][:, 0:TT], AF.Copy), s_S)
                setmark(f"ccev_{k}_{n}", s_S)

        def s_daexp_part(s, k, n0, n1):
            p = k % 2
            for n in range(n0, n1):
                if n >= 2:
                    wait(s, f"scn_{k}_{n-2}")
                elif k >= 1:
                    wait(s, f"scn_{k-1}_{14+n}")
                for c in range(8):
                    act = s.activation(da2[n % 2][:, c, :], deltab[p][:, c, :], AF.Exp,
                                       scale=aneg[:, c, n:n + 1])
                inc(act, s_S)
                setmark(f"da_{k}_{n}", s_S)

        def sec_scalar(s):
            wait(s, "prolog_ld")
            wait(s, "epscv")
            inc(s.activation(xn[:, :, 0:16], aneg[:, :, :], AF.Exp), s_S)
            setmark("anegexp", s_S)
            for k in range(n_tiles):
                if k >= 1:
                    s_softplus(s, k - 1)
                    s_daexp_part(s, k - 1, 0, 2)
                s_front_a(s, k)
                if k >= 1:
                    s_daexp_part(s, k - 1, 2, 6)
                s_front_b(s, k)
                if k >= 1:
                    s_daexp_part(s, k - 1, 6, 10)
                s_silu(s, k)
                if k >= 1:
                    s_daexp_part(s, k - 1, 10, 16)
            s_softplus(s, n_tiles - 1)
            s_daexp_part(s, n_tiles - 1, 0, 16)
            for g_i in range(NTG):
                for gg in range(8):
                    gr = g_i * 8 + gg
                    if gr >= n_tiles:
                        continue
                    wait(s, f"xagv_{gr}")
                    if gr == 0:
                        wait(s, "p1v")
                    if gr >= 1:
                        wait(s, f"p2uf_{gr-1}")
                        wait(s, f"p2glog_{gr-1}")
                        wait(s, f"p2statrow_{gr-1}")
                    if g_i >= 1 and gg == 0:
                        wait(s, f"ystore_{g_i-1}")
                    for c in range(8):
                        act = s.activation(acc[:, gg, c * 128:(c + 1) * 128],
                                           xag[:, c, :], AF.Square)
                    inc(act, s_S)
                    setmark(f"p2xsq_{gr}", s_S)
                    wait(s, f"p2sumx_{gr}")
                    wait(s, f"p2sumx2_{gr}")
                    s.activation(statrow[0:1, 0:HT], q_pre[0][0:1, 0:HT], AF.Copy,
                                 scale=1.0 / 1024)
                    s.activation(tiny[0:1, 0:HT], q_pre[0][0:1, HT:2 * HT], AF.Copy,
                                 scale=1.0 / 1024)
                    s.drain()
                    inc(s.activation(tiny[0:1, HT:2 * HT], statrow[0:1, 0:HT],
                                     AF.Square), s_S)
                    setmark(f"p2stA_{gr}", s_S)
                    wait(s, f"p2stB_{gr}")
                    inc(s.activation(tiny[0:1, 0:HT], tiny[0:1, HT:2 * HT], AF.Sqrt,
                                     bias=epsc[0:1, 0:1]), s_S)
                    setmark(f"p2stC_{gr}", s_S)
                    wait(s, f"p2negl_{gr}")
                    inc(s.activation(g8[:, 128:256], g8[:, 0:128], AF.Exp), s_S)
                    setmark(f"p2ge_{gr}", s_S)
                for e in range(E):
                    gbase = g_i * 8
                    for tt in range(4):
                        if gbase + tt * 2 >= n_tiles:
                            continue
                        for m in range(32):
                            wait(s, f"mm1_{g_i}_{e}_{tt}_{m}")
                            if g_i == 0 and e == 0 and tt == 0 and m == 0:
                                wait(s, "p2consts")
                            act = s.activation(h2[:, m, :], q_mm1[m % 2][:, 0:256],
                                               AF.Gelu, bias=b1[:, e, m:m + 1])
                            inc(act, s_S)
                            setmark(f"geev_{g_i}_{e}_{tt}_{m}", s_S)
                        setmark(f"gedone_{g_i}_{e}_{tt}", s_S)

        def v_front(v, k):
            p = k % 2
            if k == 0:
                inc(v.memset(xihalo[:, :, 0:3], 0.0), s_V)
            else:
                wait(v, f"ipev_{k-1}_{7}")
                inc(v.tensor_copy(xihalo[:, :, 0:3], xihalo[:, :, TT:TT + 3]), s_V)
            setmark(f"halo_{k}", s_V)
            wait(v, f"stA_{k}")
            inc(v.tensor_sub(tiny[0:1, TT:2 * TT], tiny[0:1, 0:TT],
                             tiny[0:1, TT:2 * TT]), s_V)
            setmark(f"stB_{k}", s_V)
            wait(v, f"stC_{k}")
            inc(v.reciprocal(statrow[0:1, TT:2 * TT], tiny[0:1, 0:TT]), s_V)
            setmark(f"statrow_{k}", s_V)
            wait(v, f"bcast1_{k}")
            wait(v, f"sumx2_{k}")
            if k >= 1:
                wait(v, f"xcbfs_{k-1}")
            for c in range(8):
                v.tensor_sub(xn[:, c, :], x_sb[p][:, c, :], pp[1][:, 0:TT])
            v.drain()
            for c in range(8):
                vv = v.tensor_mul(xn[:, c, :], xn[:, c, :], pp[1][:, TT:2 * TT])
            inc(vv, s_V)
            setmark(f"lnapply_{k}", s_V)
            wait(v, f"rmsrow_{k}")
            inc(v.reciprocal(statrow[0:1, 256:256 + TT], tiny[0:1, 0:TT]), s_V)
            setmark(f"rrmsrow_{k}", s_V)
            wait(v, f"bcast2_{k}")
            for c in range(8):
                vv = v.scalar_tensor_tensor(xn2bf[:, c, :], xn[:, c, :],
                                            normw[:, c:c + 1], pp[0][:, TT:2 * TT],
                                            ALU.mult, ALU.mult)
            inc(vv, s_V)
            setmark(f"xn2v_{k}", s_V)
            wait(v, f"sumxn2_{k}")
            for c in range(8):
                v.tensor_add(rpre[:, c, :], x_sb[p][:, c, :], xn[:, c, :])
            v.drain()
            for c in range(8):
                vv = v.tensor_scalar(rpre[:, c, :], rpre[:, c, :], 0.5, None, ALU.mult)
            inc(vv, s_V)
            setmark(f"rprev_{k}", s_V)
        def v_front2(v, k):
            p = k % 2
            if k >= 1:
                wait(v, f"dblstore_{k-1}")
            for c in range(8):
                wait(v, f"ipev_{k}_{c}")
                v.tensor_scalar(ctmp[0][:, :], xihalo[:, c, 0:TT], convw[:, c, 0:1],
                                None, ALU.mult)
                v.drain()
                v.scalar_tensor_tensor(ctmp[1][:, :], xihalo[:, c, 1:1 + TT],
                                       convw[:, c, 1:2], ctmp[0][:, :], ALU.mult, ALU.add)
                v.drain()
                v.scalar_tensor_tensor(ctmp[0][:, :], xihalo[:, c, 2:2 + TT],
                                       convw[:, c, 2:3], ctmp[1][:, :], ALU.mult, ALU.add)
                v.drain()
                vv = v.scalar_tensor_tensor(convo[:, c, :], xihalo[:, c, 3:3 + TT],
                                            convw[:, c, 3:4], ctmp[0][:, :],
                                            ALU.mult, ALU.add)
                v.drain()
            inc(vv, s_V)
            setmark(f"convt_{k}", s_V)
            wait(v, f"dblps_{k}")
            inc(v.tensor_copy(ctmp[0][0:96, :], pp[3][0:96, 0:TT]), s_V)
            setmark(f"dblmy_{k}", s_V)


        def v_scan_head(v, k):
            p = k % 2
            wait(v, f"delta_{k}")
            for c in range(8):
                vv = v.tensor_mul(wbf[p][:, c, :], deltab[p][:, c, :], xcbf[p][:, c, :])
            inc(vv, s_V)
            setmark(f"wk_{k}", s_V)

        def v_scan_part(v, k, n0, n1):
            p = k % 2
            for n in range(n0, n1):
                pn = n % 2
                wait(v, f"da_{k}_{n}")
                wait(v, f"dbu_{k}_{n}")
                for c in range(8):
                    v.tensor_tensor_scan(h_b[:, c, :], da2[pn][:, c, :],
                                         dbu2[pn][:, c * TT:(c + 1) * TT],
                                         hstate[:, c, n:n + 1], ALU.mult, ALU.add)
                v.drain()
                v.tensor_copy(hstate[:, :, n:n + 1], h_b[:, :, TT - 1:TT])
                for c in range(8):
                    vv = v.tensor_mul(dbu2[pn][:, c * TT:(c + 1) * TT],
                                      h_b[:, c, :], crep[:, n, :])
                v.drain()
                inc(vv, s_V)
                setmark(f"scn_{k}_{n}", s_V)

        def v_scan_tail(v, k):
            p = k % 2
            if k >= 1:
                wait(v, f"op_{k-1}_{7}")
            wait(v, f"ym_{k}_{15}")
            for g in range(4):
                for cc in range(2):
                    c = 2 * g + cc
                    vv = v.tensor_mul(ybf[:, c, :], pp[YB[g]][:, cc * TT:(cc + 1) * TT],
                                      szbf[p][:, c, :])
            v.drain()
            inc(vv, s_V)
            setmark(f"ydone_{k}", s_V)
            setmark(f"scandone_{k}", s_V)

        def v_opart_prev(v, k):
            for dm in range(8):
                bid = 3 + (dm % 2)
                wait(v, f"op_{k}_{dm}")
                if dm == 0 and k >= 1:
                    wait(v, f"opstore_{k-1}")
                v.tensor_copy(opart[:, dm, 0, :], pp[bid][:, 0:HT])
                inc(v.tensor_copy(opart[:, dm, 1, :], pp[bid][:, HT:TT]), s_V)
                setmark(f"opev_{k}_{dm}", s_V)
            setmark(f"opartv_{k}", s_V)

        def sec_vector(v):
            inc(v.memset(epsc[0:1, 0:1], EPS), s_V)
            setmark("epscv", s_V)
            wait(v, "anegexp")
            inc(v.tensor_scalar(aneg[:, :, :], xn[:, :, 0:16], -1.0, None, ALU.mult),
                s_V)
            setmark("anegdone", s_V)
            for c in range(8):
                vv = v.tensor_scalar(dpd[:, c, :], idb[:, :], dp[:, c:c + 1],
                                     None, ALU.mult)
            v.drain()
            inc(vv, s_V)
            setmark("dpdv", s_V)
            inc(v.memset(hstate.ap(), 0.0), s_V)
            for k in range(n_tiles):
                if k >= 2:
                    v_opart_prev(v, k - 2)
                v_front(v, k)
                if k >= 1:
                    v_scan_head(v, k - 1)
                    v_scan_part(v, k - 1, 0, 8)
                v_front2(v, k)
                if k >= 1:
                    v_scan_part(v, k - 1, 8, 16)
                    v_scan_tail(v, k - 1)
            if n_tiles >= 2:
                v_opart_prev(v, n_tiles - 2)
            v_scan_head(v, n_tiles - 1)
            v_scan_part(v, n_tiles - 1, 0, 16)
            v_scan_tail(v, n_tiles - 1)
            v_opart_prev(v, n_tiles - 1)
            inc(v.engine_nop(), s_V)
            setmark("p1v", s_V)
            wait(v, f"rprestore_{n_tiles-1}")
            wait(v, f"opstore_{n_tiles-1}")
            if n_tiles % 8:
                v.memset(u_bf.ap(), 0.0)
                v.drain()
            for g_i in range(NTG):
                for gg in range(8):
                    gr = g_i * 8 + gg
                    if gr >= n_tiles:
                        continue
                    wait(v, f"xagv_{gr}")
                    wait(v, f"p2stA_{gr}")
                    inc(v.tensor_sub(tiny[0:1, HT:2 * HT], tiny[0:1, 0:HT],
                                     tiny[0:1, HT:2 * HT]), s_V)
                    setmark(f"p2stB_{gr}", s_V)
                    wait(v, f"p2stC_{gr}")
                    inc(v.reciprocal(statrow[0:1, HT:2 * HT], tiny[0:1, 0:HT]), s_V)
                    setmark(f"p2statrow_{gr}", s_V)
                    wait(v, f"p2bcast_{gr}")
                    wait(v, f"p2sumx2_{gr}")
                    for c in range(8):
                        v.tensor_sub(acc[:, gg, c * 128:(c + 1) * 128], xag[:, c, :],
                                     q_pre[1][:, 0:HT])
                    v.drain()
                    for c in range(8):
                        vv = v.tensor_mul(acc[:, gg, c * 128:(c + 1) * 128],
                                          acc[:, gg, c * 128:(c + 1) * 128],
                                          q_pre[1][:, HT:2 * HT])
                    inc(vv, s_V)
                    setmark(f"p2lnapply_{gr}", s_V)
                    v.drain()
                    if g_i >= 1 and gg == 0:
                        wait(v, f"mm1d_{g_i-1}_{E-1}")
                    for c in range(8):
                        vv = v.tensor_copy(u_bf[:, c, gg * 128:(gg + 1) * 128],
                                           acc[:, gg, c * 128:(c + 1) * 128])
                    inc(vv, s_V)
                    setmark(f"p2uf_{gr}", s_V)
                    wait(v, f"p2glog_{gr}")
                    v.tensor_scalar(g8[:, 0:128], q_pre[0][0:8, 2 * HT:3 * HT],
                                    -1.0, 30.0, ALU.mult, ALU.min)
                    v.drain()
                    inc(v.engine_nop(), s_V)
                    setmark(f"p2negl_{gr}", s_V)
                    wait(v, f"p2ge_{gr}")
                    v.tensor_scalar(g8[:, 0:128], g8[:, 128:256], 1.0, 1e12,
                                    ALU.add, ALU.min)
                    v.drain()
                    v.reciprocal(g8[:, 128:256], g8[:, 0:128])
                    v.drain()
                    inc(v.engine_nop(), s_V)
                    setmark(f"p2g8_{gr}", s_V)
                    wait(v, f"p2gt_{gr}")
                    gsl = q_pre[0][:, 3 * HT:3 * HT + 8]
                    v.tensor_copy(gtm[:, 5, :], gsl)
                    v.drain()
                    v.max(gtm[:, 0, :], gtm[:, 5, :])
                    v.drain()
                    v.max_index(gtmi[:, :], gtm[:, 0, :], gtm[:, 5, :])
                    v.drain()
                    v.tensor_copy(gtm[:, 1, :], gtmi[:, :])
                    v.drain()
                    v.tensor_scalar(gtm[:, 2, :], iota8[:, :], gtm[:, 1, 0:1], None,
                                    ALU.is_equal)
                    v.tensor_scalar(gtm[:, 3, :], iota8[:, :], gtm[:, 1, 1:2], None,
                                    ALU.is_equal)
                    v.drain()
                    v.tensor_add(gtm[:, 2, :], gtm[:, 2, :], gtm[:, 3, :])
                    v.tensor_add(gtm[:, 4, 0:1], gtm[:, 0, 0:1], gtm[:, 0, 1:2])
                    v.drain()
                    v.tensor_mul(gtm[:, 3, :], gtm[:, 5, :], gtm[:, 2, :])
                    v.reciprocal(gtm[:, 4, 1:2], gtm[:, 4, 0:1])
                    v.drain()
                    vv = v.tensor_scalar(gn[:, gr, :], gtm[:, 3, :], gtm[:, 4, 1:2],
                                         None, ALU.mult)
                    inc(vv, s_V)
                    setmark(f"p2gn_{gr}", s_V)
                    wait(v, f"p2gnT_{gr}")
                    if gr >= 1:
                        wait(v, f"p2b2_{gr-1}")
                    inc(v.tensor_copy(gnT[:, :], q_pre[0][0:8, HT:2 * HT]), s_V)
                    setmark(f"p2gnTv_{gr}", s_V)
                    if g_i >= 1 and gg == 0:
                        wait(v, f"ystore_{g_i-1}")
                    wait(v, f"p2b2_{gr}")
                    for dm in range(8):
                        wait(v, f"p2tr_{gr}_{dm}")
                        v.tensor_copy(acc[:, gg, dm * 128:(dm + 1) * 128],
                                      q_pre[dm % 2][:, 2 * HT:3 * HT])
                        v.drain()
                        vv = v.tensor_add(acc[:, gg, dm * 128:(dm + 1) * 128],
                                          acc[:, gg, dm * 128:(dm + 1) * 128],
                                          q_mm2[0][:, dm * 128:(dm + 1) * 128])
                        inc(vv, s_V)
                        setmark(f"p2ai_{gr}_{dm}", s_V)
                        v.drain()
                    setmark(f"accinit_{g_i}_{gg}", s_V)
                for e in range(E):
                    for gg in range(8):
                        gr = g_i * 8 + gg
                        if gr >= n_tiles:
                            continue
                        wait(v, f"mm2_{g_i}_{e}_{gg}")
                        vv = v.scalar_tensor_tensor(acc[:, gg, :], q_mm2[gg % 2][:, 0:1024],
                                                    gn[:, gr, e:e + 1], acc[:, gg, :],
                                                    ALU.mult, ALU.add)
                        inc(vv, s_V)
                        setmark(f"accev_{g_i}_{e}_{gg}", s_V)
                        v.drain()
                setmark(f"accdone_{g_i}", s_V)

        if real:
            with nc.Block() as block:
                @block.sync
                def _(sy):
                    sec_sync(sy)

                @block.gpsimd
                def _(gp):
                    sec_gp(gp)

                @block.tensor
                def _(t):
                    sec_tensor(t)

                @block.scalar
                def _(s):
                    sec_scalar(s)

                @block.vector
                def _(v):
                    sec_vector(v)
        else:
            fake = _FakeEng()
            sec_sync(fake)
            sec_gp(fake)
            sec_tensor(fake)
            sec_scalar(fake)
            sec_vector(fake)

    nc._dbg = dict(u_bf=u_bf, gn=gn, h2=h2, acc=acc, xag=xag, w1=w1, w2=w2,
                   gnT=gnT, g8=g8, b2sb=b2, centT=centT, b1sb=b1)
    run_pass(False)
    totals1 = {s.name: s.n for s in all_sems}
    marks1 = dict(marks)
    run_pass(True)
    totals2 = {s.name: s.n for s in all_sems}
    for k_ in totals1:
        assert totals1[k_] == totals2[k_], (
            f"pass mismatch sem {k_}: pass1={totals1[k_]} pass2={totals2[k_]}")
    for k_, v_ in marks1.items():
        assert marks[k_] == v_, f"mark mismatch {k_}: pass1={v_} pass2={marks[k_]}"
    import builtins
    builtins._LAST_MARKS = {k2: (str(v2[0]), v2[1]) for k2, v2 in marks.items()}
    return nc



def _prep_core_inputs(c, x, norm_w, in_proj_w, conv_w, conv_b, x_proj_w,
                      dt_proj_w, dt_proj_b, A_log, Dp, out_proj_w,
                      centroids, W1, b1, W2, b2, n_tiles=16):
    b, j = c // 2, c % 2
    f32 = np.float32
    sl = slice(j * 1024, (j + 1) * 1024)

    xb = np.ascontiguousarray(x[b].T).astype(f32)
    x_perm = np.ascontiguousarray(
        xb.reshape(8, 128, n_tiles, TT).transpose(1, 2, 0, 3))

    ipT = in_proj_w.T
    w_in = np.concatenate(
        [ipT[:, sl], ipT[:, 2048 + j * 1024:2048 + (j + 1) * 1024]], 1)
    w_in = np.ascontiguousarray(
        w_in.reshape(8, 128, 2048).transpose(1, 0, 2)).astype(f32)

    xpT = np.ascontiguousarray(
        x_proj_w.T[sl].reshape(8, 128, 96).transpose(1, 0, 2)).astype(f32)
    dtT = np.ascontiguousarray(dt_proj_w.T[:, sl].reshape(64, 8, 128)).astype(f32)
    opT = np.ascontiguousarray(
        out_proj_w.T[sl].reshape(8, 128, 1024).transpose(1, 0, 2)).astype(f32)

    def vec8(v):
        return np.ascontiguousarray(v[sl].reshape(8, 128).T).astype(f32)

    convw_h = np.ascontiguousarray(
        conv_w[sl].reshape(8, 128, 4).transpose(1, 0, 2)).astype(f32)
    alog_h = np.ascontiguousarray(
        A_log[sl].reshape(8, 128, 16).transpose(1, 0, 2)).astype(f32)
    normw_h = np.ascontiguousarray(norm_w.reshape(8, 128).T).astype(f32)
    cT = np.ascontiguousarray(
        centroids.T.reshape(8, 128, 8).transpose(1, 0, 2)).astype(f32)
    w1h = np.ascontiguousarray(
        W1.reshape(8, 8, 128, 4096).transpose(0, 2, 1, 3)).astype(f32)
    w2h = np.ascontiguousarray(
        W2.reshape(8, 32, 128, 1024).transpose(0, 2, 1, 3)).astype(f32)
    b1h = np.ascontiguousarray(b1.reshape(8, 32, 128).transpose(2, 0, 1)).astype(f32)

    oh16 = np.zeros((16, 16, 128), dtype=np.float32)
    for n in range(16):
        oh16[n, n, :] = 1.0
    oh16 = np.ascontiguousarray(oh16.transpose(1, 0, 2)).astype(ml_dtypes.bfloat16)

    return {
        "x_in": x_perm,
        "w_in": w_in,
        "xprojT": xpT,
        "dtprojT": dtT,
        "outprojT": opT,
        "convw": convw_h,
        "convb": vec8(conv_b),
        "dtb": vec8(dt_proj_b),
        "alog": alog_h,
        "dp_in": vec8(Dp),
        "normw": normw_h,
        "centT": cT,
        "w1_in": w1h,
        "w2_in": w2h,
        "b1_in": b1h,
        "b2_in": np.ascontiguousarray(b2).astype(f32),
        "onesc": np.ones((128, 1), dtype=f32),
        "onesr": np.ones((1, 128), dtype=f32),
        "onesrb": np.ones((1, 128), dtype=ml_dtypes.bfloat16),
        "oh_in": oh16,
        "id_in": np.eye(128, dtype=f32),
        "idb_in": np.eye(128, dtype=ml_dtypes.bfloat16),
        "iota8": np.broadcast_to(np.arange(8, dtype=f32), (128, 8)).copy(),
    }


_NC_CACHE = {}


def kernel(L0_attn_x, L0_attn_norm_w, L0_attn_in_proj_w, L0_attn_conv_w,
           L0_attn_conv_b, L0_attn_x_proj_w, L0_attn_dt_proj_w,
           L0_attn_dt_proj_b, L0_attn_A_log, L0_attn_Dp, L0_attn_out_proj_w,
           L0_ffn_centroids, L0_ffn_W1, L0_ffn_b1, L0_ffn_W2, L0_ffn_b2):
    n_tiles = 16
    key = ("full", n_tiles)
    if key not in _NC_CACHE:
        _NC_CACHE[key] = build_nc(n_tiles=n_tiles)
    nc = _NC_CACHE[key]

    args = (np.asarray(L0_attn_x, np.float32), np.asarray(L0_attn_norm_w, np.float32),
            np.asarray(L0_attn_in_proj_w, np.float32),
            np.asarray(L0_attn_conv_w, np.float32),
            np.asarray(L0_attn_conv_b, np.float32),
            np.asarray(L0_attn_x_proj_w, np.float32),
            np.asarray(L0_attn_dt_proj_w, np.float32),
            np.asarray(L0_attn_dt_proj_b, np.float32),
            np.asarray(L0_attn_A_log, np.float32), np.asarray(L0_attn_Dp, np.float32),
            np.asarray(L0_attn_out_proj_w, np.float32),
            np.asarray(L0_ffn_centroids, np.float32),
            np.asarray(L0_ffn_W1, np.float32), np.asarray(L0_ffn_b1, np.float32),
            np.asarray(L0_ffn_W2, np.float32), np.asarray(L0_ffn_b2, np.float32))

    in_maps = [_prep_core_inputs(c, *args, n_tiles=n_tiles) for c in range(8)]
    res = run_bass_kernel_spmd(nc, in_maps, core_ids=list(range(8)))

    out = np.empty((B_, L_, D_), dtype=np.float32)
    for c in range(8):
        b, j = c // 2, c % 2
        ym = res.results[c]["y_my"].reshape(n_tiles, HT, 1024)
        out.reshape(B_, n_tiles, 2, HT, D_)[b, :, j, :, :] = ym
    return out
